# version 03 (final)
# baseline (speedup 1.0000x reference)
import numpy as np
import concourse.bass as bass
import concourse.mybir as mybir
from concourse.bass_utils import run_bass_kernel_spmd
from concourse.library_overlay import lower_extended_insts
from concourse import library_config as libcfg
from contextlib import ExitStack

F32 = mybir.dt.float32
BF16 = mybir.dt.bfloat16
I16 = mybir.dt.int16
I32 = mybir.dt.int32
U32 = mybir.dt.uint32
ALU = mybir.AluOpType
AF = mybir.ActivationFunctionType
AX = mybir.AxisListType
ENG = mybir.EngineType
ds = bass.ds

NC_ = 8
D = 1024
DC = 8
H = 16
T = 4096
NCH = 32
NTL = 8
TPT = 512
CPT = 4
E = 16
NF = 32
CAP = 1024
NJ = CAP // 128
CAPF = CAP // 16
CAPG = 5120
SLOTS = 2 * T
DUMMY = float(SLOTS)
EPS1 = 1e-5
EPSA = 1e-3
EPSQ = 1e-6
EPSG = 1e-9

ENGS = ("sync", "scalar", "vector", "tensor", "gpsimd")
HAZARD = {"vector": True, "scalar": True, "tensor": False, "gpsimd": False,
          "sync": False}

DBG = False
TRUNC = None


class Build:
    STREAM_K = {"ld": 4, "wld": 4, "moe": 4}
    DEFAULT_K = 2

    def __init__(self, nc, ctx):
        self.nc = nc
        self.ctx = ctx
        self.steps = {e: [] for e in ENGS}
        self.esem = {e: ctx.enter_context(nc.semaphore(f"es_{e}")) for e in ENGS}
        self.dsem = {}
        self.resv = {}
        self.rt = {}
        self.segidx = {}
        self.lasttok = {}
        self.emit_q = {}
        self.pending_guards = []

    def _ds(self, name, slot):
        key = (name, slot)
        if key not in self.dsem:
            self.dsem[key] = self.ctx.enter_context(
                self.nc.semaphore(f"d_{name}{slot}"))
            self.resv[key] = 0
            self.rt[key] = 0
        return self.dsem[key]

    def reserve(self, name, n=1):
        k = self.STREAM_K.get(name, self.DEFAULT_K)
        s = self.segidx.get(name, 0)
        self.segidx[name] = s + 1
        slot = s % k
        sem = self._ds(name, slot)
        guard = self.lasttok.get((name, slot))
        if guard is not None:
            self.pending_guards.append(guard)
        self.resv[(name, slot)] += 16 * n
        tokv = (sem, self.resv[(name, slot)])
        self.lasttok[(name, slot)] = tokv
        self.emit_q.setdefault(name, []).append([slot, n])
        return tokv

    def _emit_slot(self, name):
        q = self.emit_q[name]
        assert q and q[0][1] > 0, f"stream {name}: dma beyond reservation"
        slot = q[0][0]
        q[0][1] -= 1
        if q[0][1] == 0:
            q.pop(0)
        self.rt[(name, slot)] += 16
        return self._ds(name, slot)

    def dma(self, e, name, out, in_, **kw):
        s = self._emit_slot(name)
        e.dma_start(out, in_, **kw).then_inc(s, 16)

    def seg_sem(self, name):
        return self._emit_slot(name)

    def wait_all(self, e, name):
        k = self.STREAM_K.get(name, self.DEFAULT_K)
        for slot in range(k):
            if (name, slot) in self.resv and self.resv[(name, slot)]:
                e.wait_ge(self.dsem[(name, slot)], self.resv[(name, slot)])

    gseq = 0

    def step(self, eng, after=(), name=""):
        def deco(fn):
            waits = [t for t in after if t is not None] + self.pending_guards
            self.pending_guards = []
            Build.gseq += 1
            self.steps[eng].append((fn, waits, name, Build.gseq))
            return (self.esem[eng], len(self.steps[eng]))
        return deco

    def tk(self, eng):
        return (self.esem[eng], len(self.steps[eng]))

    def emit(self):
        nc = self.nc
        with nc.Block() as blk:
            for engname in ENGS:
                steps = self.steps[engname]
                if not steps:
                    continue

                def body(e, _steps=steps, _eng=engname):
                    waited = {}
                    own = self.esem[_eng]
                    for (fn, waits, name, gq) in _steps:
                        if TRUNC is not None and gq > TRUNC:
                            continue
                        for (sem, val) in waits:
                            if sem is own and not HAZARD[_eng]:
                                continue
                            if waited.get(id(sem), -1) >= val:
                                continue
                            e.wait_ge(sem, val)
                            waited[id(sem)] = val
                        fn(e, self)
                        e.drain().then_inc(own, 1)

                getattr(blk, engname)(body)
        if TRUNC is None:
            for k in self.resv:
                assert self.resv[k] == self.rt[k], \
                    f"dma stream {k}: reserved {self.resv[k]} != emitted {self.rt[k]}"
            assert not self.pending_guards, "unconsumed dma segment guards"


def build_nc():
    nc = bass.Bass("TRN2", target_bir_lowering=False)

    x_in = nc.dram_tensor("x", [T, D], F32, kind="ExternalInput").ap()
    wqT_d = nc.dram_tensor("wqT", [DC, DC, 128, 128], F32, kind="ExternalInput").ap()
    wkT_d = nc.dram_tensor("wkT", [DC, DC, 128, 128], F32, kind="ExternalInput").ap()
    wvT_d = nc.dram_tensor("wvT", [DC, DC, 128, 128], F32, kind="ExternalInput").ap()
    wq2_d = nc.dram_tensor("wq2", [DC, 128, D], F32, kind="ExternalInput").ap()
    wk2_d = nc.dram_tensor("wk2", [DC, 128, D], F32, kind="ExternalInput").ap()
    wv2_d = nc.dram_tensor("wv2", [DC, 128, D], F32, kind="ExternalInput").ap()
    wo_d = nc.dram_tensor("wo", [DC, 128, D], F32, kind="ExternalInput").ap()
    bqkv_d = nc.dram_tensor("bqkv", [128, 4 * DC], F32, kind="ExternalInput").ap()
    rows_d = nc.dram_tensor("rows4", [4, D], F32, kind="ExternalInput").ap()
    gamma_d = nc.dram_tensor("gamma_row", [1, D], F32, kind="ExternalInput").ap()
    beta_d = nc.dram_tensor("beta_row", [1, D], F32, kind="ExternalInput").ap()
    wrt_d = nc.dram_tensor("wrt", [128, DC, E], F32, kind="ExternalInput").ap()
    rb_d = nc.dram_tensor("rb_col", [E, 1], F32, kind="ExternalInput").ap()
    w1_d = nc.dram_tensor("w1t", [E, NF, 128, DC, 128], F32, kind="ExternalInput").ap()
    w2_d = nc.dram_tensor("w2t", [E, NF, 128, D], F32, kind="ExternalInput").ap()
    b1_d = nc.dram_tensor("b1_pp", [128, E, NF], F32, kind="ExternalInput").ap()
    b2_d = nc.dram_tensor("b2_rows", [E, D], F32, kind="ExternalInput").ap()
    maskLT_d = nc.dram_tensor("maskLT", [1, NC_], F32, kind="ExternalInput").ap()
    rk4_d = nc.dram_tensor("rk4", [1, 1], U32, kind="ExternalInput").ap()

    y_out = nc.dram_tensor("y", [T, D], F32, kind="ExternalOutput").ap()
    dbg = {}
    if DBG:
        for nm, shp, dt in [("xmid", [T, D], F32), ("logits", [128, NCH, H], F32),
                            ("route", [128, 6, NCH], F32), ("counts", [1, NC_, E], F32),
                            ("goff", [1, E], F32), ("S", [128, DC * 128], BF16),
                            ("qks", [128, NCH, H], F32), ("gslot", [32, 256], F32),
                            ("comp", [16, E, CAPF], F32), ("xn2", [T, D], BF16),
                            ("weff", [3, 128, DC, D], BF16)]:
            dbg[nm] = nc.dram_tensor("dbg_" + nm, shp, dt, kind="ExternalOutput").ap()

    xn1_hbm = nc.dram_tensor("xn1_hbm", [DC, 128, T], BF16).ap()
    bvp_hbm = nc.dram_tensor("bvp_hbm", [1, D], F32).ap()
    weff_hbm = nc.dram_tensor("weff_hbm", [3, 128, DC, D], BF16).ap()
    xmid_hbm = nc.dram_tensor("xmid_hbm", [T, D], F32).ap()
    xn2_hbm = nc.dram_tensor("xn2_hbm", [2 * T + 128, D], BF16).ap()
    osl_hbm = nc.dram_tensor("oslots_hbm", [SLOTS + 128, D], BF16).ap()

    pid_ap = nc.partition_id_tensor.ap()

    gctx = ExitStack()
    Build.gseq = 0
    bld = Build(nc, gctx)

    def sb(name, shape, dtype):
        return gctx.enter_context(nc.sbuf_tensor(name, shape, dtype))

    rsem = gctx.enter_context(nc.semaphore("rsem"))
    lsem = gctx.enter_context(nc.semaphore("lsem"))
    rsem2 = gctx.enter_context(nc.semaphore("rsem2"))
    lsem2 = gctx.enter_context(nc.semaphore("lsem2"))
    psem = gctx.enter_context(nc.semaphore("psem"))

    stats = sb("stats", [128, 8, NCH], F32)
    qks = sb("qks", [128, NCH, H], F32)
    r_tm = sb("r_tm", [128, NCH, H], F32)
    S_bf = sb("S_bf", [128, DC, 128], BF16)
    logits_tm = sb("logits_tm", [128, NCH, H], F32)
    onesb = sb("onesb", [1, 128], BF16)
    onesf = sb("onesf", [1, 128], F32)
    onescol0 = sb("onescol0", [128, 128], BF16)
    beffv_pp = sb("beffv_pp", [128, DC], F32)
    biaspp = sb("biaspp", [128, 4 * DC], F32)
    biasppb = sb("biasppb", [128, 4 * DC], BF16)
    b1pp = sb("b1pp", [128, E, NF], F32)
    wrt_sb = sb("wrt_sb", [128, DC, E], F32)
    rb_sb = sb("rb_sb", [16, 1], F32)
    maskLT_sb = sb("maskLT_sb", [1, NC_], F32)
    rk_sb = sb("rk_sb", [1, 1], U32)
    rk4_sb = sb("rk4_sb", [1, 1], U32)
    iot16 = sb("iot16", [128, H], F32)
    iotB = sb("iotB", [128, H], F32)
    ident_b = sb("ident_b", [128, 128], BF16)
    ident_f = sb("ident_f", [128, 128], F32)
    tri32 = sb("tri32", [32, 32], F32)
    selp = sb("selp", [16, DC, 128], BF16)
    v1c = sb("v1c", [128, NCH], F32)
    v2c = sb("v2c", [128, NCH], F32)
    i1c = sb("i1c", [128, NCH], F32)
    i2c = sb("i2c", [128, NCH], F32)
    g1c = sb("g1c", [128, NCH], F32)
    g2c = sb("g2c", [128, NCH], F32)
    sc16 = sb("sc16", [128, H], F32)
    sc16b = sb("sc16b", [128, H], F32)
    stt1 = sb("stt1", [128, NCH], F32)
    stt2 = sb("stt2", [128, NCH], F32)
    i1T = sb("i1T", [32, 128], F32)
    i2T = sb("i2T", [32, 128], F32)
    g1T = sb("g1T", [32, 128], F32)
    g2T = sb("g2T", [32, 128], F32)
    zer32 = sb("zer32", [32, 256], F32)
    siota = sb("siota", [32, 256], F32)
    comp_all = sb("comp_all", [16, E, CAPF], F32)
    spill_pad = sb("spill_pad", [16, 256], F32)
    counts_u = sb("counts_u", [1, E], U32)
    counts_fx = sb("counts_fx", [128, 64], F32)
    counts_bc = sb("counts_bc", [16, E], F32)
    counts_ag = sb("counts_ag", [128, NC_, 64], F32)
    goff_row = sb("goff_row", [1, E], F32)
    wiota = sb("wiota", [16, CAPF], F32)
    mskf = sb("mskf", [16, CAPF], F32)
    mski = sb("mski", [16, CAPF], I32)
    csafe = sb("csafe", [16, CAPF], F32)
    slot_idx = sb("slot_idx", [128, E, CAPF], I16)
    gslot = sb("gslot", [32, 256], F32)
    kacc = sb("kacc", [32, 256], F32)
    g0k = sb("g0k", [128, NCH], F32)
    g1k = sb("g1k", [128, NCH], F32)

    tok = {}

    pctx = ExitStack()
    gam_b = pctx.enter_context(nc.sbuf_tensor("gam_b", [128, D], F32))
    bet_b = pctx.enter_context(nc.sbuf_tensor("bet_b", [128, D], F32))
    beff_rows = pctx.enter_context(nc.sbuf_tensor("beff_rows", [1, 4, D], BF16))

    tok["setup_ld"] = bld.reserve("ld", 7)

    @bld.step("sync", name="setup_loads")
    def _(e, b):
        b.dma(e, "ld", biaspp[:, :], bqkv_d)
        b.dma(e, "ld", b1pp[:, :, :], b1_d)
        b.dma(e, "ld", wrt_sb[:, :, :], wrt_d)
        b.dma(e, "ld", maskLT_sb[:, :], maskLT_d)
        b.dma(e, "ld", rk_sb[:, :], pid_ap)
        b.dma(e, "ld", rk4_sb[:, :], rk4_d)
        b.dma(e, "ld", rb_sb[:, :], rb_d)
        return None

    @bld.step("gpsimd", name="setup_gp")
    def _(e, b):
        e.iota(iot16[:, :], [[1, H]], base=0, channel_multiplier=0,
               allow_small_or_imprecise_dtypes=True)
        e.iota(iotB[:, :], [[1, H]], base=2 ** 20, channel_multiplier=0,
               allow_small_or_imprecise_dtypes=True)
        e.iota(ident_f[:, :], [[1, 128]], base=0, channel_multiplier=-1,
               allow_small_or_imprecise_dtypes=True)
        e.drain()
        e.tensor_single_scalar(ident_f[:, :], ident_f[:, :], 0.0, ALU.is_equal)
        e.drain()
        e.tensor_copy(ident_b[:, :], ident_f[:, :])
        e.iota(tri32[:, :], [[1, 32]], base=0, channel_multiplier=-1,
               allow_small_or_imprecise_dtypes=True)
        e.drain()
        e.tensor_single_scalar(tri32[:, :], tri32[:, :], 0.0, ALU.is_gt)
        e.iota(siota[:, :], [[1, 256]], base=0, channel_multiplier=256,
               allow_small_or_imprecise_dtypes=True)
        e.iota(wiota[:, :], [[16, CAPF]], base=0, channel_multiplier=1,
               allow_small_or_imprecise_dtypes=True)
        e.iota(selp[:, :, :], [[0, DC], [1, 2], [0, 64]], base=0,
               channel_multiplier=-1, allow_small_or_imprecise_dtypes=True)
        e.drain()
        for p in range(DC):
            e.tensor_single_scalar(selp[:, p, :], selp[:, p, :], float(-2 * p),
                                   ALU.is_equal)
        e.memset(zer32[:, :], 0.0)
        e.memset(onesf[:, :], 1.0)
        e.drain()
        e.tensor_copy(onesb[:, :], onesf[:, :])
        e.memset(onescol0[:, :], 0.0)
        e.drain()
        e.memset(onescol0[0:1, :], 1.0)
        e.memset(slot_idx[:, :, :], 0)
        return e.nop()
    tok["setup_gp0"] = bld.tk("gpsimd")

    @bld.step("gpsimd", after=(tok["setup_ld"],), name="biasppb_cvt")
    def _(e, b):
        return e.tensor_copy(biasppb[:, :], biaspp[:, :])
    tok["setup_gp"] = bld.tk("gpsimd")
    tok["biasppb"] = tok["setup_gp"]

    sctx = ExitStack()
    gb_rows = sctx.enter_context(nc.sbuf_tensor("gb_rows", [1, 2, D], F32))
    bvrow_f = sctx.enter_context(nc.sbuf_tensor("bvrow_f", [1, D], F32))
    zrow = sctx.enter_context(nc.sbuf_tensor("zrow", [128, D], BF16))
    zrowf = sctx.enter_context(nc.sbuf_tensor("zrowf", [128, 64], F32))
    rows_raw = sctx.enter_context(nc.sbuf_tensor("rows_raw", [1, 4, D], BF16))
    s_ps = ExitStack()
    ps_set = s_ps.enter_context(nc.psum_tensor("ps_set", [128, 512], F32))

    tok["gbld"] = bld.reserve("ld", 2)

    @bld.step("sync", name="gbld")
    def _(e, b):
        b.dma(e, "ld", gb_rows[0:1, 0, :], gamma_d)
        b.dma(e, "ld", gb_rows[0:1, 1, :], beta_d)
        return None

    tok["rows_raw"] = bld.reserve("gld", 1)

    @bld.step("gpsimd", name="rows_raw_ld")
    def _(e, b):
        b.dma(e, "gld", rows_raw[0:1, :, :], rows_d)
        return None

    prev_ps_set = None
    for gi in range(4):
        dst = gam_b if gi < 2 else bet_b
        row = 0 if gi < 2 else 1
        half = gi % 2

        @bld.step("tensor", after=(tok["gbld"], tok["setup_gp"], prev_ps_set),
                  name=f"gbmm{gi}")
        def _(e, b, _r=row, _h=half):
            return e.matmul(ps_set[:, :], onesf[0:1, :],
                            gb_rows[0:1, _r, _h * 512:(_h + 1) * 512],
                            start=True, stop=True)
        tok[f"gbmm{gi}"] = bld.tk("tensor")

        @bld.step("vector", after=(tok[f"gbmm{gi}"],), name=f"gbev{gi}")
        def _(e, b, _d=dst, _h=half):
            return e.tensor_copy(_d[:, _h * 512:(_h + 1) * 512], ps_set[:, :])
        tok[f"gbev{gi}"] = bld.tk("vector")
        prev_ps_set = tok[f"gbev{gi}"]
    tok["gb_done"] = tok["gbev3"]

    @bld.step("vector", name="zrow")
    def _(e, b):
        e.memset(zrow[:, :], 0.0)
        e.drain()
        return e.memset(zrowf[:, :], 0.0)
    tok["zrow"] = bld.tk("vector")

    tok["zero_osl"] = bld.reserve("zst", (SLOTS // 128 + 1) + 1)

    @bld.step("scalar", after=(tok["zrow"],), name="zero_osl")
    def _(e, b):
        for cc in range(SLOTS // 128 + 1):
            b.dma(e, "zst", osl_hbm[cc * 128:(cc + 1) * 128, :], zrow[:, :])
        b.dma(e, "zst", xn2_hbm[2 * T:2 * T + 128, :], zrow[:, :])
        return None

    fctx = ExitStack()
    wT_sb = fctx.enter_context(nc.sbuf_tensor("wT_sb", [128, DC * DC, 128], BF16))
    w2p_sb = fctx.enter_context(nc.sbuf_tensor("w2p_sb", [128, DC, D], BF16))
    stage = fctx.enter_context(nc.sbuf_tensor("stage", [128, 2, D], BF16))

    for wi, (wT_d_, w2_d_, bcol) in enumerate(
            [(wqT_d, wq2_d, 0), (wkT_d, wk2_d, DC), (wvT_d, wv2_d, 2 * DC)]):
        tok[f"fld_{wi}"] = bld.reserve("wld", 2)

        @bld.step("gpsimd", after=(tok.get(f"fdone_{wi-1}"),), name=f"fld_{wi}")
        def _(e, b, _wT=wT_d_, _w2=w2_d_):
            b.dma(e, "wld", wT_sb.ap().rearrange("p (m i) f -> p m i f", i=DC),
                  _wT.rearrange("m i p f -> p m i f"))
            b.dma(e, "wld", w2p_sb[:, :, :], _w2.rearrange("c p d -> p c d"))
            return None

        prev_fev = prev_ps_set
        for ic in range(DC):
            for hf in range(2):
                @bld.step("tensor", after=(tok[f"fld_{wi}"], prev_fev),
                          name=f"fmm_{wi}_{ic}_{hf}")
                def _(e, b, _ic=ic, _hf=hf):
                    last = None
                    sl = slice(_hf * 512, (_hf + 1) * 512)
                    for mc in range(DC):
                        last = e.matmul(ps_set[:, :], wT_sb[:, mc * DC + _ic, :],
                                        w2p_sb[:, mc, sl],
                                        start=(mc == 0), stop=(mc == DC - 1))
                    return last
                tok[f"fmm_{wi}_{ic}_{hf}"] = bld.tk("tensor")

                @bld.step("vector", after=(tok[f"fmm_{wi}_{ic}_{hf}"],
                                           tok.get(f"fstd_{wi}_{ic-2}"),
                                           (tok.get(f"fstd_{wi-1}_{DC-2+ic}")
                                            if ic < 2 else None)),
                          name=f"fev_{wi}_{ic}_{hf}")
                def _(e, b, _ic=ic, _hf=hf):
                    return e.tensor_copy(stage[:, _ic % 2, _hf * 512:(_hf + 1) * 512],
                                         ps_set[:, :])
                tok[f"fev_{wi}_{ic}_{hf}"] = bld.tk("vector")
                prev_fev = tok[f"fev_{wi}_{ic}_{hf}"]

            tok[f"fstd_{wi}_{ic}"] = bld.reserve("xst", 1)

            @bld.step("sync", after=(tok[f"fev_{wi}_{ic}_1"],),
                      name=f"fst_{wi}_{ic}")
            def _(e, b, _wi=wi, _ic=ic):
                b.dma(e, "xst", weff_hbm[_wi, :, _ic, :], stage[:, _ic % 2, :])
                return None

        for hf in range(2):
            @bld.step("tensor", after=(tok[f"fld_{wi}"], tok["biasppb"],
                                       tok["rows_raw"], prev_fev,
                                       tok.get(f"bfev_{wi}_{hf-1}")),
                      name=f"bfmm_{wi}_{hf}")
            def _(e, b, _bc=bcol, _wi=wi, _hf=hf):
                sl = slice(_hf * 512, (_hf + 1) * 512)
                last = None
                for mc in range(DC):
                    last = e.matmul(ps_set[0:1, :],
                                    biasppb[:, _bc + mc:_bc + mc + 1],
                                    w2p_sb[:, mc, sl], start=(mc == 0),
                                    stop=(mc == DC - 1))
                return last
            tok[f"bfmm_{wi}_{hf}"] = bld.tk("tensor")

            @bld.step("vector", after=(tok[f"bfmm_{wi}_{hf}"],),
                      name=f"bfev_{wi}_{hf}")
            def _(e, b, _wi=wi, _hf=hf):
                sl = slice(_hf * 512, (_hf + 1) * 512)
                e.tensor_tensor(beff_rows[0:1, _wi, sl], ps_set[0:1, :],
                                rows_raw[0:1, _wi, sl], ALU.add)
                if _wi == 2:
                    e.drain()
                    return e.tensor_tensor(bvrow_f[0:1, sl], ps_set[0:1, :],
                                           rows_raw[0:1, _wi, sl], ALU.add)
                return None
            tok[f"bfev_{wi}_{hf}"] = bld.tk("vector")
        tok[f"fdone_{wi}"] = tok[f"bfev_{wi}_1"]
        prev_ps_set = tok[f"bfev_{wi}_1"]

    @bld.step("vector", after=(tok["rows_raw"], tok["fdone_2"]), name="bo_copy")
    def _(e, b):
        return e.tensor_copy(beff_rows[0:1, 3, :], rows_raw[0:1, 3, :])
    tok["bo_copy"] = bld.tk("vector")

    tok["bvst"] = bld.reserve("xst", 1)

    @bld.step("sync", after=(tok["fdone_2"], tok["bo_copy"]), name="bvst")
    def _(e, b):
        b.dma(e, "xst", bvp_hbm[0:1, :], bvrow_f[0:1, :])
        return None

    tok["bvld"] = bld.reserve("ld", 1)

    @bld.step("sync", after=(tok["bvst"],), name="bvld")
    def _(e, b):
        b.dma(e, "ld", beffv_pp[:, :],
              bvp_hbm.rearrange("o (c p) -> (o p) c", p=128),
              allow_slow_non_contiguous=True)
        return None
    tok["beffv_done"] = tok["bvld"]

    fctx.close()
    s_ps.close()
    sctx.close()

    actx = ExitStack()

    def asb(name, shape, dtype):
        return actx.enter_context(nc.sbuf_tensor(name, shape, dtype))

    x_t = asb("x_t", [128, CPT, D], F32)
    xnb = asb("xnb", [128, 2, D], BF16)
    xnfm = asb("xnfm", [128, DC, TPT], BF16)
    q2p = asb("q2p", [128, 2, D], BF16)
    k2p = asb("k2p", [128, 2, D], BF16)
    tmin = asb("tmin", [128, D], F32)
    texp = asb("texp", [128, D], F32)
    sjunk = asb("sjunk", [128, D], F32)
    ybuf = asb("ybuf", [128, CPT, D], F32)
    zbuf = asb("zbuf", [128, D], F32)
    vtmp = asb("vtmp", [128, D], F32)
    x2fm = asb("x2fm", [128, DC, 128], F32)
    lgfm = asb("lgfm", [16, 128], F32)
    v2fm = asb("v2fm", [128, DC, TPT], BF16)
    v2r = asb("v2r", [128, 2, TPT], BF16)
    outsb = asb("outsb", [128, DC, TPT], BF16)
    r_fm = asb("r_fm", [16, T], BF16)
    agS = asb("agS", [128, 4, DC * 128], BF16)
    S_loc = asb("S_loc", [128, DC * 128], F32)
    S_snd = asb("S_snd", [128, DC * 128], BF16)

    a_ps = ExitStack()
    ps_qv = a_ps.enter_context(nc.psum_tensor("ps_qv", [128, 512], F32))
    ps_q2 = a_ps.enter_context(nc.psum_tensor("ps_q2", [128, 1024], F32))
    p1_ps = ExitStack()
    ps_tp = p1_ps.enter_context(nc.psum_tensor("ps_tp", [128, 2, 4, 128], BF16))
    ps_S = p1_ps.enter_context(nc.psum_tensor("ps_S", [128, DC, 128], F32))

    w1ctx = ExitStack()
    weffq = w1ctx.enter_context(nc.sbuf_tensor("weffq", [128, DC, D], BF16))
    weffk = w1ctx.enter_context(nc.sbuf_tensor("weffk", [128, DC, D], BF16))

    tok["p1_wload"] = bld.reserve("ld", 2 * DC)

    @bld.step("sync", after=(tok[f"fstd_0_{DC-1}"], tok[f"fstd_0_{DC-2}"],
                             tok[f"fstd_1_{DC-1}"], tok[f"fstd_1_{DC-2}"],
                             tok["beffv_done"], tok[f"fstd_2_{DC-1}"],
                             tok[f"fstd_2_{DC-2}"]),
              name="p1_wload")
    def _(e, b):
        for fi in range(DC):
            b.dma(e, "ld", weffq[:, fi, :], weff_hbm[0, :, fi, :])
            b.dma(e, "ld", weffk[:, fi, :], weff_hbm[1, :, fi, :])
        return None

    for t in range(NTL):
        tok[f"xload_{t}"] = bld.reserve("ld", CPT)

        @bld.step("sync", after=(tok.get(f"xn_{t-1}_{CPT-1}"),
                                 (tok["beffv_done"] if t == 0 else None),
                                 (tok[f"fstd_2_{DC-1}"] if t == 0 else None),
                                 (tok[f"fstd_2_{DC-2}"] if t == 0 else None),
                                 (tok["zero_osl"] if t == 0 else None)),
                  name=f"xload_{t}")
        def _(e, b, _t=t):
            for c in range(CPT):
                b.dma(e, "ld", x_t[:, c, :],
                      x_in[(_t * CPT + c) * 128:(_t * CPT + c + 1) * 128, :])
            return None

        @bld.step("scalar", after=(tok[f"xload_{t}"], tok.get(f"var1_{t-1}")),
                  name=f"st1_{t}")
        def _(e, b, _t=t):
            last = None
            for c in range(CPT):
                ch = _t * CPT + c
                e.activation(sjunk[:, :], x_t[:, c, :], AF.Identity,
                             bias=0.0, scale=1.0, accum_out=stats[:, 6, ch:ch + 1])
                e.drain()
                last = e.activation(sjunk[:, :], x_t[:, c, :], AF.Square,
                                    bias=0.0, scale=1.0,
                                    accum_out=stats[:, 7, ch:ch + 1])
                e.drain()
            return last
        tok[f"st1_{t}"] = bld.tk("scalar")

        @bld.step("vector", after=(tok[f"st1_{t}"],), name=f"var1_{t}")
        def _(e, b, _t=t):
            sl = slice(_t * CPT, (_t + 1) * CPT)
            e.tensor_scalar(stats[:, 0, sl], stats[:, 6, sl], 1.0 / D, None, ALU.mult)
            e.drain()
            e.tensor_scalar(stt1[:, 0:CPT], stats[:, 7, sl], 1.0 / D, None, ALU.mult)
            e.drain()
            e.tensor_tensor(stt2[:, 0:CPT], stats[:, 0, sl], stats[:, 0, sl],
                            ALU.mult)
            e.drain()
            e.tensor_tensor(stt1[:, 0:CPT], stt1[:, 0:CPT], stt2[:, 0:CPT],
                            ALU.subtract)
            e.drain()
            return e.tensor_scalar(stt1[:, 0:CPT], stt1[:, 0:CPT], 1.0, EPS1,
                                   ALU.mult, ALU.add)
        tok[f"var1_{t}"] = bld.tk("vector")

        @bld.step("scalar", after=(tok[f"var1_{t}"],), name=f"sq1_{t}")
        def _(e, b):
            return e.activation(stt2[:, 0:CPT], stt1[:, 0:CPT], AF.Sqrt,
                                bias=0.0, scale=1.0)
        tok[f"sq1_{t}"] = bld.tk("scalar")

        @bld.step("vector", after=(tok[f"sq1_{t}"],), name=f"rs1_{t}")
        def _(e, b, _t=t):
            return e.reciprocal(stats[:, 1, _t * CPT:(_t + 1) * CPT], stt2[:, 0:CPT])
        tok[f"rs1_{t}"] = bld.tk("vector")

        for c in range(CPT):
            gch = t * CPT + c

            @bld.step("vector", after=(tok[f"rs1_{t}"],
                                       tok.get(f"tpb_g{gch-2}")),
                      name=f"xn_{t}_{c}")
            def _(e, b, _t=t, _c=c):
                ch = _t * CPT + _c
                return e.tensor_scalar(xnb[:, _c % 2, :], x_t[:, _c, :],
                                       stats[:, 0, ch:ch + 1],
                                       stats[:, 1, ch:ch + 1],
                                       ALU.subtract, ALU.mult)
            tok[f"xn_{t}_{c}"] = bld.tk("vector")

            @bld.step("tensor", after=(tok[f"xn_{t}_{c}"], tok["setup_gp"],
                                       tok.get(f"tpeb_g{gch-2}"),
                                       ),
                      name=f"tpa_{t}_{c}")
            def _(e, b, _c=c):
                last = None
                for k in range(4):
                    last = e.transpose(ps_tp[:, _c % 2, k, :],
                                       xnb[:, _c % 2, k * 128:(k + 1) * 128],
                                       ident_b[:, :])
                return last
            tok[f"tpa_{t}_{c}"] = bld.tk("tensor")

            @bld.step("vector", after=(tok[f"tpa_{t}_{c}"],
                                       tok.get(f"q2mm_{t-1}_{c}_1"),
                                       (tok.get(f"xnst_{t-1}") if c == 0 else None)),
                      name=f"tpea_{t}_{c}")
            def _(e, b, _t=t, _c=c):
                dst = xnfm[:, 0:4, _c * 128:(_c + 1) * 128]
                return e.tensor_copy(dst, ps_tp[:, _c % 2, :, :])
            tok[f"tpea_{t}_{c}"] = bld.tk("vector")

            @bld.step("tensor", after=(tok[f"tpea_{t}_{c}"],), name=f"tpb_{t}_{c}")
            def _(e, b, _c=c):
                last = None
                for k in range(4):
                    last = e.transpose(ps_tp[:, _c % 2, k, :],
                                       xnb[:, _c % 2, (4 + k) * 128:(5 + k) * 128],
                                       ident_b[:, :])
                return last
            tok[f"tpb_{t}_{c}"] = bld.tk("tensor")
            tok[f"tpb_g{gch}"] = tok[f"tpb_{t}_{c}"]

            @bld.step("vector", after=(tok[f"tpb_{t}_{c}"],), name=f"tpeb_{t}_{c}")
            def _(e, b, _t=t, _c=c):
                dst = xnfm[:, 4:8, _c * 128:(_c + 1) * 128]
                return e.tensor_copy(dst, ps_tp[:, _c % 2, :, :])
            tok[f"tpeb_{t}_{c}"] = bld.tk("vector")
            tok[f"tpeb_g{gch}"] = tok[f"tpeb_{t}_{c}"]

            for qk in range(2):
                weff = weffq if qk == 0 else weffk
                dst2 = q2p if qk == 0 else k2p

                @bld.step("tensor",
                          after=(tok[f"tpeb_{t}_{c}"], tok[f"fdone_{qk}"],
                                 tok["p1_wload"],
                                 tok.get(f"elu2_g{gch}_{qk-1}"),
                                 (tok.get(f"elu2_g{gch-1}_1") if qk == 0 else None),
                                 tok.get(f"Smm_g{gch-2}"),
                                 tok.get(f"qks_g{gch-2}")),
                          name=f"q2mm_{t}_{c}_{qk}")
                def _(e, b, _t=t, _c=c, _ri=qk, _w=weff):
                    last = None
                    for half in range(2):
                        sl = slice(half * 512, (half + 1) * 512)
                        for fi in range(DC):
                            e.matmul(ps_q2[:, sl],
                                     xnfm[:, fi, _c * 128:(_c + 1) * 128],
                                     _w[:, fi, sl], start=(fi == 0), stop=False)
                        last = e.matmul(ps_q2[:, sl], onesb[0:1, :],
                                        beff_rows[0:1, _ri, sl], start=False,
                                        stop=True)
                    return last
                tok[f"q2mm_{t}_{c}_{qk}"] = bld.tk("tensor")
                tok[f"q2mm_g{gch}_{qk}"] = tok[f"q2mm_{t}_{c}_{qk}"]

                @bld.step("vector", after=(tok[f"q2mm_{t}_{c}_{qk}"],),
                          name=f"elu0_{t}_{c}_{qk}")
                def _(e, b):
                    return e.tensor_single_scalar(tmin[:, :], ps_q2[:, :], 0.0,
                                                  ALU.min)
                tok[f"elu0_{t}_{c}_{qk}"] = bld.tk("vector")

                @bld.step("scalar", after=(tok[f"elu0_{t}_{c}_{qk}"],),
                          name=f"elu1_{t}_{c}_{qk}")
                def _(e, b):
                    return e.activation(texp[:, :], tmin[:, :], AF.Exp,
                                        bias=0.0, scale=1.0)
                tok[f"elu1_{t}_{c}_{qk}"] = bld.tk("scalar")

                @bld.step("vector", after=(tok[f"elu1_{t}_{c}_{qk}"],),
                          name=f"elu2_{t}_{c}_{qk}")
                def _(e, b, _dst=dst2, _c=c):
                    return e.scalar_tensor_tensor(_dst[:, _c % 2, :], ps_q2[:, :],
                                                  0.0, texp[:, :], ALU.max, ALU.add)
                tok[f"elu2_{t}_{c}_{qk}"] = bld.tk("vector")
                tok[f"elu2_g{gch}_{qk}"] = tok[f"elu2_{t}_{c}_{qk}"]

            @bld.step("vector", after=(tok[f"elu2_g{gch}_0"], tok[f"elu2_g{gch}_1"]),
                      name=f"qks_{t}_{c}")
            def _(e, b, _t=t, _c=c):
                ch = _t * CPT + _c
                e.tensor_tensor(vtmp[:, :], q2p[:, _c % 2, :], k2p[:, _c % 2, :],
                                ALU.mult)
                e.drain()
                return e.tensor_reduce(qks[:, ch, :],
                                       vtmp.rearrange("p (h d) -> p h d", d=64),
                                       AX.X, ALU.add)
            tok[f"qks_{t}_{c}"] = bld.tk("vector")
            tok[f"qks_g{gch}"] = tok[f"qks_{t}_{c}"]

            @bld.step("tensor", after=(tok[f"elu2_g{gch}_0"], tok[f"elu2_g{gch}_1"]),
                      name=f"Smm_{t}_{c}")
            def _(e, b, _t=t, _c=c):
                last = None
                for p in range(DC):
                    first = (_t == 0 and _c == 0 and p % 4 == 0)
                    lastg = (_t == NTL - 1 and _c == CPT - 1 and p % 4 == 3)
                    last = e.matmul(ps_S[:, p, :],
                                    k2p[:, _c % 2, p * 128:(p + 1) * 128],
                                    q2p[:, _c % 2, p * 128:(p + 1) * 128],
                                    start=first, stop=lastg, skip_group_check=True)
                return last
            tok[f"Smm_{t}_{c}"] = bld.tk("tensor")
            tok[f"Smm_g{gch}"] = tok[f"Smm_{t}_{c}"]

        tok[f"xnst_{t}"] = bld.reserve("xst", DC)

        @bld.step("sync", after=(tok[f"tpeb_{t}_{CPT-1}"],), name=f"xnst_{t}")
        def _(e, b, _t=t):
            for fi in range(DC):
                b.dma(e, "xst", xn1_hbm[fi, :, _t * TPT:(_t + 1) * TPT],
                      xnfm[:, fi, :])
            return None

    tok["S_done"] = tok[f"Smm_{NTL-1}_{CPT-1}"]

    @bld.step("vector", after=(tok["S_done"],), name="Sev")
    def _(e, b):
        return e.tensor_copy(S_snd.rearrange("p (a c) -> p a c", c=128),
                             ps_S[:, :, :])
    tok["Sev"] = bld.tk("vector")

    @bld.step("gpsimd", after=(tok["Sev"], tok["setup_ld"]), name="xchS")
    def _(e, b):
        e.load_library(libcfg.remote_dma)
        rank4 = nc.values_load(rk4_sb[0:1, 0:1], engines=[ENG.Pool],
                               min_val=0, max_val=3)
        e.remote_dma_broadcast(
            agS[:, ds(rank4, 1), :], S_snd[:, :], rsem, lsem,
            rdests=[(0, 0), (0, 1), (0, 2), (0, 3), None, None, None, None],
        ).then_inc(psem, 1)
        e.wait_ge(psem, 1)
        e.trigger_dma(count=1)
        e.wait_ge(lsem, 16)
        e.wait_ge(rsem, 8)
        return e.nop()
    tok["xchS"] = bld.tk("gpsimd")

    @bld.step("vector", after=(tok["xchS"],), name="Ssum")
    def _(e, b):
        e.tensor_tensor(S_loc[:, :], agS[:, 0, :], agS[:, 1, :], ALU.add)
        e.drain()
        e.tensor_tensor(S_loc[:, :], S_loc[:, :], agS[:, 2, :], ALU.add)
        e.drain()
        e.tensor_tensor(S_loc[:, :], S_loc[:, :], agS[:, 3, :], ALU.add)
        e.drain()
        Sv = S_loc.rearrange("p (c f) -> p c f", f=128)
        for p in range(DC):
            e.memset(Sv[0:64, p, 64:128], 0.0)
            e.drain()
            e.memset(Sv[64:128, p, 0:64], 0.0)
            e.drain()
        return e.tensor_copy(S_bf[:, :, :], Sv)
    tok["Ssum"] = bld.tk("vector")

    p1_ps.close()
    p2_ps = ExitStack()
    ps_rb = p2_ps.enter_context(nc.psum_tensor("ps_rb", [128, 512], F32))
    ps_out = p2_ps.enter_context(nc.psum_tensor("ps_out", [128, 512], F32))
    ps_rtt = p2_ps.enter_context(nc.psum_tensor("ps_rtt", [128, 4, 128], F32))
    ps_lg = p2_ps.enter_context(nc.psum_tensor("ps_lg", [16, 128], F32))

    w1ctx.close()
    w2ctx = ExitStack()
    weffv = w2ctx.enter_context(nc.sbuf_tensor("weffv", [128, DC, D], BF16))
    wo_sb = w2ctx.enter_context(nc.sbuf_tensor("wo_sb", [128, DC, D], BF16))

    tok["p2_wload"] = bld.reserve("ld", DC)

    @bld.step("sync", after=(tok[f"fstd_2_{DC-1}"], tok[f"fstd_2_{DC-2}"],
                             tok[f"q2mm_{NTL-1}_{CPT-1}_1"]),
              name="p2_wload")
    def _(e, b):
        for fi in range(DC):
            b.dma(e, "ld", weffv[:, fi, :], weff_hbm[2, :, fi, :])
        return None

    tok["wo_ld"] = bld.reserve("wld", 1)

    @bld.step("gpsimd", after=(tok["xchS"], tok[f"q2mm_{NTL-1}_{CPT-1}_1"]),
              name="wo_ld")
    def _(e, b):
        b.dma(e, "wld", wo_sb[:, :, :], wo_d.rearrange("c p d -> p c d"))
        return None

    @bld.step("vector", after=(tok[f"qks_{NTL-1}_{CPT-1}"],), name="rprep")
    def _(e, b):
        e.tensor_scalar(r_tm[:, :, :], qks[:, :, :], EPSQ, None, ALU.add)
        e.drain()
        return e.reciprocal(r_tm[:, :, :], r_tm[:, :, :])
    tok["rprep"] = bld.tk("vector")

    for g in range(NCH // 4):
        @bld.step("tensor", after=(tok["rprep"], tok.get(f"rTe_{g-1}"),
                                   tok["Ssum"]), name=f"rT_{g}")
        def _(e, b, _g=g):
            last = None
            for k in range(4):
                ch = _g * 4 + k
                last = e.transpose(ps_rtt[0:16, k, :], r_tm[:, ch, :],
                                   ident_f[:, :])
            return last
        tok[f"rT_{g}"] = bld.tk("tensor")

        @bld.step("vector", after=(tok[f"rT_{g}"],), name=f"rTe_{g}")
        def _(e, b, _g=g):
            last = None
            for k in range(4):
                ch = _g * 4 + k
                last = e.tensor_copy(r_fm[:, ch * 128:(ch + 1) * 128],
                                     ps_rtt[0:16, k, :])
                e.drain()
            return last
        tok[f"rTe_{g}"] = bld.tk("vector")
    tok["r_fm"] = tok[f"rTe_{NCH//4-1}"]

    for t in range(NTL):
        tok[f"xl2_{t}"] = bld.reserve("ld", CPT + DC)

        @bld.step("sync", after=(tok.get(f"xm_{t-1}_{CPT-1}"),
                                 tok.get(f"v2mm_{t-1}_{DC-1}"),
                                 tok[f"xnst_{t}"], tok["Sev"],
                                 (tok[f"xnst_{NTL-1}"] if t == 0 else None),
                                 tok["p2_wload"]), name=f"xl2_{t}")
        def _(e, b, _t=t):
            for c in range(CPT):
                b.dma(e, "ld", x_t[:, c, :],
                      x_in[(_t * CPT + c) * 128:(_t * CPT + c + 1) * 128, :])
            for fi in range(DC):
                b.dma(e, "ld", xnfm[:, fi, :],
                      xn1_hbm[fi, :, _t * TPT:(_t + 1) * TPT])
            return None

        prev_v = tok.get(f"v2ev_{t-1}_{DC-1}")
        for fo in range(DC):
            @bld.step("tensor", after=(tok[f"xl2_{t}"], prev_v), name=f"v2mm_{t}_{fo}")
            def _(e, b, _t=t, _fo=fo):
                last = None
                for fi in range(DC):
                    last = e.matmul(ps_qv[:, :],
                                    weffv[:, fi, _fo * 128:(_fo + 1) * 128],
                                    xnfm[:, fi, :],
                                    start=(fi == 0), stop=(fi == DC - 1))
                return last
            tok[f"v2mm_{t}_{fo}"] = bld.tk("tensor")

            @bld.step("scalar", after=(tok[f"v2mm_{t}_{fo}"],
                                       tok.get(f"v2r_{t-1}_{fo}")),
                      name=f"v2ev_{t}_{fo}")
            def _(e, b, _fo=fo):
                return e.activation(v2fm[:, _fo, :], ps_qv[:, :], AF.Identity,
                                    bias=beffv_pp[:, _fo:_fo + 1], scale=1.0)
            tok[f"v2ev_{t}_{fo}"] = bld.tk("scalar")
            prev_v = tok[f"v2ev_{t}_{fo}"]

        for p in range(DC):
            @bld.step("tensor", after=(tok["r_fm"], tok[f"v2ev_{t}_{p}"],
                                       tok.get(f"v2r_{t}_{p-1}"),
                                       (tok.get(f"v2r_{t-1}_{DC-1}") if p == 0
                                        else None)),
                      name=f"rbmm_{t}_{p}")
            def _(e, b, _t=t, _p=p):
                return e.matmul(ps_rb[:, :], selp[:, _p, :],
                                r_fm[:, _t * TPT:(_t + 1) * TPT],
                                start=True, stop=True)
            tok[f"rbmm_{t}_{p}"] = bld.tk("tensor")

            if p >= 2:
                war_v2r = tok.get(f"outmm_{t}_{p-2}")
            else:
                war_v2r = tok.get(f"outmm_{t-1}_{DC-2+p}")

            @bld.step("vector", after=(tok[f"rbmm_{t}_{p}"], war_v2r),
                      name=f"v2r_{t}_{p}")
            def _(e, b, _p=p):
                return e.tensor_tensor(v2r[:, _p % 2, :], v2fm[:, _p, :],
                                       ps_rb[:, :], ALU.mult)
            tok[f"v2r_{t}_{p}"] = bld.tk("vector")

            @bld.step("tensor", after=(tok[f"v2r_{t}_{p}"], tok["Ssum"],
                                       tok.get(f"outev_{t}_{p-1}"),
                                       (tok.get(f"womm_{t-1}_{CPT-1}") if p == 0
                                        else None),
                                       (tok.get(f"outev_{t-1}_{DC-1}") if p == 0
                                        else None)),
                      name=f"outmm_{t}_{p}")
            def _(e, b, _p=p):
                return e.matmul(ps_out[:, :], S_bf[:, _p, :], v2r[:, _p % 2, :],
                                start=True, stop=True)
            tok[f"outmm_{t}_{p}"] = bld.tk("tensor")

            @bld.step("vector", after=(tok[f"outmm_{t}_{p}"],), name=f"outev_{t}_{p}")
            def _(e, b, _p=p):
                return e.tensor_copy(outsb[:, _p, :], ps_out[:, :])
            tok[f"outev_{t}_{p}"] = bld.tk("vector")
        tok[f"out_{t}"] = tok[f"outev_{t}_{DC-1}"]

        for c in range(CPT):
            @bld.step("tensor", after=(tok[f"out_{t}"], tok["bo_copy"],
                                       tok["wo_ld"],
                                       tok.get(f"y_{t}_{c-1}"),
                                       (tok.get(f"y_{t-1}_{CPT-1}") if c == 0
                                        else None)),
                      name=f"womm_{t}_{c}")
            def _(e, b, _c=c):
                last = None
                for half in range(2):
                    sl = slice(half * 512, (half + 1) * 512)
                    for ec in range(DC):
                        e.matmul(ps_q2[:, sl],
                                 outsb[:, ec, _c * 128:(_c + 1) * 128],
                                 wo_sb[:, ec, sl], start=(ec == 0), stop=False)
                    last = e.matmul(ps_q2[:, sl], onesb[0:1, :],
                                    beff_rows[0:1, 3, sl], start=False, stop=True)
                return last
            tok[f"womm_{t}_{c}"] = bld.tk("tensor")

            @bld.step("vector", after=(tok[f"womm_{t}_{c}"],
                                       tok.get(f"xn2_{t-1}_{c}"),
                                       tok.get(f"xmst_{t-1}_{c}")),
                      name=f"y_{t}_{c}")
            def _(e, b, _t=t, _c=c):
                ch = _t * CPT + _c
                e.tensor_scalar(vtmp[:, :], x_t[:, _c, :],
                                stats[:, 0, ch:ch + 1], stats[:, 1, ch:ch + 1],
                                ALU.subtract, ALU.mult)
                e.drain()
                return e.tensor_tensor(ybuf[:, _c, :], ps_q2[:, :], vtmp[:, :],
                                       ALU.add)
            tok[f"y_{t}_{c}"] = bld.tk("vector")

        @bld.step("scalar", after=(tok[f"y_{t}_{CPT-1}"],
                                   tok.get(f"var1_{NTL-1}") if t == 0
                                   else tok.get(f"var2_{t-1}")), name=f"sta_{t}")
        def _(e, b, _t=t):
            last = None
            for c in range(CPT):
                ch = _t * CPT + c
                e.activation(sjunk[:, :], ybuf[:, c, :], AF.Identity,
                             bias=0.0, scale=1.0, accum_out=stats[:, 6, ch:ch + 1])
                e.drain()
                last = e.activation(sjunk[:, :], ybuf[:, c, :], AF.Square,
                                    bias=0.0, scale=1.0,
                                    accum_out=stats[:, 7, ch:ch + 1])
                e.drain()
            return last
        tok[f"sta_{t}"] = bld.tk("scalar")

        @bld.step("vector", after=(tok[f"sta_{t}"],), name=f"vara_{t}")
        def _(e, b, _t=t):
            sl = slice(_t * CPT, (_t + 1) * CPT)
            e.tensor_scalar(stats[:, 2, sl], stats[:, 6, sl], 1.0 / D, None,
                            ALU.mult)
            e.drain()
            e.tensor_scalar(stt1[:, 0:CPT], stats[:, 7, sl], 1.0 / D, None, ALU.mult)
            e.drain()
            e.tensor_tensor(stt2[:, 0:CPT], stats[:, 2, sl], stats[:, 2, sl],
                            ALU.mult)
            e.drain()
            e.tensor_tensor(stt1[:, 0:CPT], stt1[:, 0:CPT], stt2[:, 0:CPT],
                            ALU.subtract)
            e.drain()
            return e.tensor_scalar(stt1[:, 0:CPT], stt1[:, 0:CPT], 1.0, EPSA,
                                   ALU.mult, ALU.add)
        tok[f"vara_{t}"] = bld.tk("vector")

        @bld.step("scalar", after=(tok[f"vara_{t}"],), name=f"sqa_{t}")
        def _(e, b):
            return e.activation(stt2[:, 0:CPT], stt1[:, 0:CPT], AF.Sqrt,
                                bias=0.0, scale=1.0)
        tok[f"sqa_{t}"] = bld.tk("scalar")

        @bld.step("vector", after=(tok[f"sqa_{t}"],), name=f"rsa_{t}")
        def _(e, b, _t=t):
            return e.reciprocal(stats[:, 3, _t * CPT:(_t + 1) * CPT], stt2[:, 0:CPT])
        tok[f"rsa_{t}"] = bld.tk("vector")

        for c in range(CPT):
            @bld.step("vector", after=(tok[f"rsa_{t}"], tok["gb_done"]),
                      name=f"xm_{t}_{c}")
            def _(e, b, _t=t, _c=c):
                ch = _t * CPT + _c
                e.tensor_scalar(zbuf[:, :], ybuf[:, _c, :],
                                stats[:, 2, ch:ch + 1], stats[:, 3, ch:ch + 1],
                                ALU.subtract, ALU.mult)
                e.drain()
                e.tensor_tensor(zbuf[:, :], zbuf[:, :], gam_b[:, :], ALU.mult)
                e.drain()
                e.tensor_tensor(zbuf[:, :], zbuf[:, :], bet_b[:, :], ALU.add)
                e.drain()
                return e.tensor_tensor(ybuf[:, _c, :], x_t[:, _c, :],
                                       zbuf[:, :], ALU.add)
            tok[f"xm_{t}_{c}"] = bld.tk("vector")

            tok[f"xmst_{t}_{c}"] = bld.reserve("xmst", 1)

            @bld.step("sync", after=(tok[f"xm_{t}_{c}"],), name=f"xmst_{t}_{c}")
            def _(e, b, _t=t, _c=c):
                ch = _t * CPT + _c
                b.dma(e, "xmst", xmid_hbm[ch * 128:(ch + 1) * 128, :],
                      ybuf[:, _c, :])
                return None

        @bld.step("scalar", after=(tok[f"xm_{t}_{CPT-1}"],), name=f"st2_{t}")
        def _(e, b, _t=t):
            last = None
            for c in range(CPT):
                ch = _t * CPT + c
                e.activation(sjunk[:, :], ybuf[:, c, :], AF.Identity,
                             bias=0.0, scale=1.0, accum_out=stats[:, 6, ch:ch + 1])
                e.drain()
                last = e.activation(sjunk[:, :], ybuf[:, c, :], AF.Square,
                                    bias=0.0, scale=1.0,
                                    accum_out=stats[:, 7, ch:ch + 1])
                e.drain()
            return last
        tok[f"st2_{t}"] = bld.tk("scalar")

        @bld.step("vector", after=(tok[f"st2_{t}"],), name=f"var2_{t}")
        def _(e, b, _t=t):
            sl = slice(_t * CPT, (_t + 1) * CPT)
            e.tensor_scalar(stats[:, 4, sl], stats[:, 6, sl], 1.0 / D, None,
                            ALU.mult)
            e.drain()
            e.tensor_scalar(stt1[:, 0:CPT], stats[:, 7, sl], 1.0 / D, None, ALU.mult)
            e.drain()
            e.tensor_tensor(stt2[:, 0:CPT], stats[:, 4, sl], stats[:, 4, sl],
                            ALU.mult)
            e.drain()
            e.tensor_tensor(stt1[:, 0:CPT], stt1[:, 0:CPT], stt2[:, 0:CPT],
                            ALU.subtract)
            e.drain()
            return e.tensor_scalar(stt1[:, 0:CPT], stt1[:, 0:CPT], 1.0, EPS1,
                                   ALU.mult, ALU.add)
        tok[f"var2_{t}"] = bld.tk("vector")

        @bld.step("scalar", after=(tok[f"var2_{t}"],), name=f"sq2_{t}")
        def _(e, b):
            return e.activation(stt2[:, 0:CPT], stt1[:, 0:CPT], AF.Sqrt,
                                bias=0.0, scale=1.0)
        tok[f"sq2_{t}"] = bld.tk("scalar")

        @bld.step("vector", after=(tok[f"sq2_{t}"],), name=f"rs2_{t}")
        def _(e, b, _t=t):
            return e.reciprocal(stats[:, 5, _t * CPT:(_t + 1) * CPT], stt2[:, 0:CPT])
        tok[f"rs2_{t}"] = bld.tk("vector")

        for c in range(CPT):
            gch = t * CPT + c

            @bld.step("vector", after=(tok[f"rs2_{t}"],
                                       tok.get(f"rtt2_g{gch-2}"),
                                       tok.get(f"x2st_g{gch-2}")),
                      name=f"xn2_{t}_{c}")
            def _(e, b, _t=t, _c=c):
                ch = _t * CPT + _c
                x2fc = tmin if _c % 2 == 0 else texp
                e.tensor_scalar(x2fc[:, :], ybuf[:, _c, :],
                                stats[:, 4, ch:ch + 1], stats[:, 5, ch:ch + 1],
                                ALU.subtract, ALU.mult)
                e.drain()
                return e.tensor_copy(xnb[:, _c % 2, :], x2fc[:, :])
            tok[f"xn2_{t}_{c}"] = bld.tk("vector")

            tok[f"x2st_{t}_{c}"] = bld.reserve("x2st", 2)
            tok[f"x2st_g{gch}"] = tok[f"x2st_{t}_{c}"]

            @bld.step("sync", after=(tok[f"xn2_{t}_{c}"],), name=f"x2st_{t}_{c}")
            def _(e, b, _t=t, _c=c):
                ch = _t * CPT + _c
                xv = xn2_hbm.rearrange("(t k) d -> t k d", k=2)
                b.dma(e, "x2st", xv[ch * 128:(ch + 1) * 128, 0, :],
                      xnb[:, _c % 2, :])
                b.dma(e, "x2st", xv[ch * 128:(ch + 1) * 128, 1, :],
                      xnb[:, _c % 2, :])
                return None

            @bld.step("tensor", after=(tok[f"xn2_{t}_{c}"],
                                       tok.get(f"lgtev_g{gch-1}"),
                                       tok.get(f"lgmm_g{gch-1}")),
                      name=f"rtt_{t}_{c}")
            def _(e, b, _c=c):
                x2fc = tmin if _c % 2 == 0 else texp
                last = None
                for k in range(4):
                    last = e.transpose(ps_rtt[:, k, :],
                                       x2fc[:, k * 128:(k + 1) * 128],
                                       ident_f[:, :])
                return last
            tok[f"rtt_{t}_{c}"] = bld.tk("tensor")

            @bld.step("vector", after=(tok[f"rtt_{t}_{c}"],), name=f"rtev0_{t}_{c}")
            def _(e, b):
                return e.tensor_copy(x2fm[:, 0:4, :], ps_rtt[:, :, :])
            tok[f"rtev0_{t}_{c}"] = bld.tk("vector")

            @bld.step("tensor", after=(tok[f"rtev0_{t}_{c}"],), name=f"rtt2_{t}_{c}")
            def _(e, b, _c=c):
                x2fc = tmin if _c % 2 == 0 else texp
                last = None
                for k in range(4):
                    last = e.transpose(ps_rtt[:, k, :],
                                       x2fc[:, (4 + k) * 128:(5 + k) * 128],
                                       ident_f[:, :])
                return last
            tok[f"rtt2_{t}_{c}"] = bld.tk("tensor")
            tok[f"rtt2_g{gch}"] = tok[f"rtt2_{t}_{c}"]

            @bld.step("vector", after=(tok[f"rtt2_{t}_{c}"],), name=f"rtev_{t}_{c}")
            def _(e, b):
                return e.tensor_copy(x2fm[:, 4:8, :], ps_rtt[:, :, :])
            tok[f"rtev_{t}_{c}"] = bld.tk("vector")

            @bld.step("tensor", after=(tok[f"rtev_{t}_{c}"], tok["setup_ld"],
                                       tok.get(f"lgev_g{gch-1}")),
                      name=f"lgmm_{t}_{c}")
            def _(e, b):
                last = None
                for fi in range(DC):
                    last = e.matmul(ps_lg[:, :], wrt_sb[:, fi, :], x2fm[:, fi, :],
                                    start=(fi == 0), stop=(fi == DC - 1))
                return last
            tok[f"lgmm_{t}_{c}"] = bld.tk("tensor")
            tok[f"lgmm_g{gch}"] = tok[f"lgmm_{t}_{c}"]

            @bld.step("scalar", after=(tok[f"lgmm_{t}_{c}"],
                                       tok.get(f"lgt_g{gch-1}")),
                      name=f"lgev_{t}_{c}")
            def _(e, b):
                return e.activation(lgfm[:, :], ps_lg[:, :], AF.Identity,
                                    bias=rb_sb[:, 0:1], scale=1.0)
            tok[f"lgev_{t}_{c}"] = bld.tk("scalar")
            tok[f"lgev_g{gch}"] = tok[f"lgev_{t}_{c}"]

            @bld.step("tensor", after=(tok[f"lgev_{t}_{c}"],), name=f"lgt_{t}_{c}")
            def _(e, b):
                return e.transpose(ps_rtt[:, 0, 0:16], lgfm[:, :], ident_f[0:16, 0:16])
            tok[f"lgt_{t}_{c}"] = bld.tk("tensor")
            tok[f"lgt_g{gch}"] = tok[f"lgt_{t}_{c}"]

            @bld.step("vector", after=(tok[f"lgt_{t}_{c}"],), name=f"lgtev_{t}_{c}")
            def _(e, b, _t=t, _c=c):
                ch = _t * CPT + _c
                return e.tensor_copy(logits_tm[:, ch, :], ps_rtt[:, 0, 0:16])
            tok[f"lgtev_{t}_{c}"] = bld.tk("vector")
            tok[f"lgtev_g{gch}"] = tok[f"lgtev_{t}_{c}"]

    tok["p2_done"] = tok[f"lgtev_{NTL-1}_{CPT-1}"]

    w2ctx.close()
    actx.close()
    pctx.close()
    p2_ps.close()
    a_ps.close()

    rctx = ExitStack()
    oh_all = rctx.enter_context(nc.sbuf_tensor("oh_all", [32, E, 256], F32))
    scan_all = rctx.enter_context(nc.sbuf_tensor("scan_all", [32, E, 256], F32))
    kcmp = rctx.enter_context(nc.sbuf_tensor("kcmp", [32, 256], F32))
    kept1 = rctx.enter_context(nc.sbuf_tensor("kept1", [32, 256], F32))
    arr32 = rctx.enter_context(nc.sbuf_tensor("arr32", [32, 256], F32))
    goffp = rctx.enter_context(nc.sbuf_tensor("goffp", [32, E], F32))
    arr16 = rctx.enter_context(nc.sbuf_tensor("arr16", [16, 512], F32))
    den32 = rctx.enter_context(nc.sbuf_tensor("den32", [32, 128], F32))
    scr32 = rctx.enter_context(nc.sbuf_tensor("scr32", [32, 128], F32))
    r_ps = ExitStack()
    ps_r1 = r_ps.enter_context(nc.psum_tensor("ps_r1", [128, 128], F32))

    guard = (tok[f"x2st_{NTL-1}_{CPT-1}"], tok[f"x2st_{NTL-1}_{CPT-2}"],
             tok[f"xmst_{NTL-1}_{CPT-1}"], tok[f"xmst_{NTL-1}_{CPT-2}"],
             tok[f"xnst_{NTL-1}"], tok["p2_done"])

    for ch in range(NCH):
        tl, cc = ch // CPT, ch % CPT

        @bld.step("vector",
                  after=(tok[f"lgtev_{tl}_{cc}"], tok["setup_gp"]) +
                        (guard if ch == 0 else ()),
                  name=f"top2_{ch}")
        def _(e, b, _ch=ch):
            lg = logits_tm[:, _ch, :]
            e.tensor_reduce(v1c[:, _ch:_ch + 1], lg, AX.X, ALU.max)
            e.drain()
            e.tensor_scalar(sc16[:, :], lg, v1c[:, _ch:_ch + 1], None, ALU.is_ge)
            e.drain()
            e.scalar_tensor_tensor(sc16b[:, :], sc16[:, :], -float(2 ** 20),
                                   iotB[:, :], ALU.mult, ALU.add)
            e.drain()
            e.tensor_reduce(i1c[:, _ch:_ch + 1], sc16b[:, :], AX.X, ALU.min)
            e.drain()
            e.tensor_scalar(sc16[:, :], iot16[:, :], i1c[:, _ch:_ch + 1], None,
                            ALU.is_equal)
            e.drain()
            e.scalar_tensor_tensor(sc16b[:, :], sc16[:, :], -1e30, lg,
                                   ALU.mult, ALU.add)
            e.drain()
            e.tensor_reduce(v2c[:, _ch:_ch + 1], sc16b[:, :], AX.X, ALU.max)
            e.drain()
            e.tensor_scalar(sc16[:, :], sc16b[:, :], v2c[:, _ch:_ch + 1], None,
                            ALU.is_ge)
            e.drain()
            e.scalar_tensor_tensor(sc16b[:, :], sc16[:, :], -float(2 ** 20),
                                   iotB[:, :], ALU.mult, ALU.add)
            e.drain()
            e.tensor_reduce(i2c[:, _ch:_ch + 1], sc16b[:, :], AX.X, ALU.min)
            e.drain()
            return None
        tok[f"top2_{ch}"] = bld.tk("vector")

    @bld.step("vector", after=(tok[f"top2_{NCH-1}"],), name="dv")
    def _(e, b):
        return e.tensor_tensor(stt1[:, :], v2c[:, :], v1c[:, :], ALU.subtract)
    tok["dv"] = bld.tk("vector")

    @bld.step("scalar", after=(tok["dv"],), name="gexp")
    def _(e, b):
        return e.activation(stt2[:, :], stt1[:, :], AF.Exp, bias=0.0, scale=1.0)
    tok["gexp"] = bld.tk("scalar")

    @bld.step("vector", after=(tok["gexp"],), name="gates")
    def _(e, b):
        e.tensor_scalar(stt1[:, :], stt2[:, :], 1.0, None, ALU.add)
        e.drain()
        e.reciprocal(g1c[:, :], stt1[:, :])
        e.drain()
        return e.tensor_tensor(g2c[:, :], stt2[:, :], g1c[:, :], ALU.mult)
    tok["gates"] = bld.tk("vector")

    for k, (src, dst) in enumerate([(i1c, i1T), (i2c, i2T), (g1c, g1T),
                                    (g2c, g2T)]):
        @bld.step("tensor", after=(tok["gates"], tok.get(f"trTe_{k-1}")),
                  name=f"trT_{k}")
        def _(e, b, _s=src):
            return e.transpose(ps_r1[0:32, :], _s[:, :], ident_f[:, :])
        tok[f"trT_{k}"] = bld.tk("tensor")

        @bld.step("vector", after=(tok[f"trT_{k}"],), name=f"trTe_{k}")
        def _(e, b, _d=dst):
            return e.tensor_copy(_d[:, :], ps_r1[0:32, :])
        tok[f"trTe_{k}"] = bld.tk("vector")
    tok["trT"] = tok["trTe_3"]

    for ee in range(E):
        @bld.step("vector", after=(tok["trT"], tok.get(f"arrD_{ee-1}")),
                  name=f"ohE_{ee}")
        def _(e, b, _e=ee):
            e.tensor_scalar(oh_all[:, _e, 0:256:2], i1T[:, :], float(_e), None,
                            ALU.is_equal)
            e.drain()
            e.tensor_scalar(oh_all[:, _e, 1:256:2], i2T[:, :], float(_e), None,
                            ALU.is_equal)
            e.drain()
            e.scalar_tensor_tensor(arr32[:, :], siota[:, :], 1.0,
                                   oh_all[:, _e, :], ALU.add, ALU.mult)
            e.drain()
            e.tensor_scalar(arr32[:, :], arr32[:, :], -1.0, None, ALU.add)
            e.drain()
            return e.tensor_tensor_scan(scan_all[:, _e, :], oh_all[:, _e, :],
                                        zer32[:, :], 0.0, ALU.add, ALU.add)
        tok[f"ohE_{ee}"] = bld.tk("vector")

        tok[f"arrD_{ee}"] = bld.reserve("gld", 2)

        @bld.step("gpsimd", after=(tok[f"ohE_{ee}"], tok["xchS"]),
                  name=f"arrD_{ee}")
        def _(e, b):
            b.dma(e, "gld", arr16[:, 0:256], arr32[0:16, :])
            b.dma(e, "gld", arr16[:, 256:512], arr32[16:32, :])
            return None

        @bld.step("gpsimd", after=(tok[f"arrD_{ee}"],), name=f"spg_{ee}")
        def _(e, b, _e=ee):
            if _e == 0:
                e.load_library(libcfg.sparse_gather)
            return e.sparse_gather(comp_all[:, _e, :], arr16[:, :],
                                   num_found=counts_u[0:1, _e:_e + 1])
        tok[f"spg_{ee}"] = bld.tk("gpsimd")

    @bld.step("vector", after=(tok[f"spg_{E-1}"],), name="cvt")
    def _(e, b):
        e.memset(counts_fx[:, :], 0.0)
        e.drain()
        return e.tensor_copy(counts_fx[0:1, 0:E], counts_u[:, :])
    tok["cvt"] = bld.tk("vector")

    @bld.step("gpsimd", after=(tok["cvt"],), name="xchC")
    def _(e, b):
        e.load_library(libcfg.remote_dma)
        rank = nc.values_load(rk_sb[0:1, 0:1], engines=[ENG.Pool],
                              min_val=0, max_val=7)
        e.remote_dma_broadcast(
            counts_ag[:, ds(rank, 1), :], counts_fx[:, :], rsem2, lsem2,
            rdests=[(0, k) for k in range(NC_)],
        ).then_inc(psem, 1)
        e.wait_ge(psem, 2)
        e.trigger_dma(count=1)
        e.wait_ge(lsem2, 16)
        e.wait_ge(rsem2, 16)
        return e.nop()
    tok["xchC"] = bld.tk("gpsimd")

    @bld.step("vector", after=(tok["xchC"],), name="goffev")
    def _(e, b):
        e.tensor_scalar(goff_row[:, :], counts_ag[0:1, 0, 0:E],
                        maskLT_sb[0:1, 0:1], None, ALU.mult)
        for r in range(1, NC_):
            e.drain()
            e.scalar_tensor_tensor(goff_row[:, :], counts_ag[0:1, r, 0:E],
                                   maskLT_sb[0:1, r:r + 1], goff_row[:, :],
                                   ALU.mult, ALU.add)
        return None
    tok["goffev"] = bld.tk("vector")

    @bld.step("gpsimd", after=(tok["goffev"], tok["xchC"]), name="goffbc")
    def _(e, b):
        e.load_library(libcfg.mlp)
        return e.partition_broadcast(goffp[:, :], goff_row[0:1, 0:E],
                                     channels=32)
    tok["goffbc"] = bld.tk("gpsimd")

    for ee in range(E):
        @bld.step("tensor", after=(tok["goffev"], tok[f"ohE_{ee}"],
                                   tok.get(f"kept_{ee-1}")), name=f"offmm_{ee}")
        def _(e, b, _e=ee):
            return e.matmul(ps_r1[0:32, 0:1], tri32[:, :],
                            scan_all[:, _e, 255:256], start=True, stop=True)
        tok[f"offmm_{ee}"] = bld.tk("tensor")

        @bld.step("vector", after=(tok[f"offmm_{ee}"], tok["goffbc"]),
                  name=f"kept_{ee}")
        def _(e, b, _e=ee):
            e.tensor_scalar(kcmp[:, :], scan_all[:, _e, :], ps_r1[0:32, 0:1],
                            None, ALU.add)
            e.drain()
            e.tensor_scalar(kcmp[:, :], kcmp[:, :], goffp[0:32, _e:_e + 1],
                            float(CAPG), ALU.add, ALU.is_le)
            e.drain()
            e.tensor_tensor(kept1[:, :], kcmp[:, :], oh_all[:, _e, :], ALU.mult)
            e.drain()
            if _e == 0:
                return e.tensor_copy(kacc[:, :], kept1[:, :])
            return e.tensor_tensor(kacc[:, :], kacc[:, :], kept1[:, :], ALU.add)
        tok[f"kept_{ee}"] = bld.tk("vector")

    @bld.step("vector", after=(tok[f"kept_{E-1}"],), name="gfin")
    def _(e, b):
        e.tensor_tensor(den32[:, :], g1T[:, :], kacc[:, 0:256:2], ALU.mult)
        e.drain()
        e.tensor_tensor(scr32[:, :], g2T[:, :], kacc[:, 1:256:2], ALU.mult)
        e.drain()
        e.tensor_tensor(den32[:, :], den32[:, :], scr32[:, :], ALU.add)
        e.drain()
        e.tensor_scalar(den32[:, :], den32[:, :], EPSG, None, ALU.add)
        e.drain()
        e.reciprocal(den32[:, :], den32[:, :])
        e.drain()
        e.tensor_tensor(gslot[:, 0:256:2], g1T[:, :], den32[:, :], ALU.mult)
        e.drain()
        e.tensor_tensor(gslot[:, 1:256:2], g2T[:, :], den32[:, :], ALU.mult)
        e.drain()
        return e.tensor_tensor(gslot[:, :], gslot[:, :], kacc[:, :], ALU.mult)
    tok["gfin"] = bld.tk("vector")

    for gk, nmk in [(0, "g0k"), (1, "g1k")]:
        @bld.step("tensor", after=(tok["gfin"], tok.get("g0kE") if gk == 1
                                   else tok["trT"]), name=f"{nmk}T")
        def _(e, b, _gk=gk):
            return e.transpose(ps_r1[:, 0:32], gslot[:, _gk:256:2],
                               ident_f[0:32, 0:32])
        tok[f"{nmk}T"] = bld.tk("tensor")

        @bld.step("vector", after=(tok[f"{nmk}T"],), name=f"{nmk}E")
        def _(e, b, _gk=gk):
            dst = g0k if _gk == 0 else g1k
            return e.tensor_copy(dst[:, :], ps_r1[:, 0:32])
        tok[f"{nmk}E"] = bld.tk("vector")
    tok["gstore"] = tok["g1kE"]

    @bld.step("gpsimd", after=(tok["cvt"], tok["xchC"]), name="cbc")
    def _(e, b):
        e.load_library(libcfg.mlp)
        return e.partition_broadcast(counts_bc[:, :], counts_fx[0:1, 0:E],
                                     channels=16)
    tok["cbc"] = bld.tk("gpsimd")

    for ee in range(E):
        @bld.step("vector", after=(tok["cbc"], tok[f"spg_{ee}"],
                                   tok.get(f"sanI_{ee-1}")), name=f"sanV_{ee}")
        def _(e, b, _e=ee):
            e.tensor_scalar(mskf[:, :], wiota[:, :], counts_bc[:, _e:_e + 1], None,
                            ALU.is_lt)
            e.drain()
            return e.memset(csafe[:, :], DUMMY)
        tok[f"sanV_{ee}"] = bld.tk("vector")

        @bld.step("scalar", after=(tok[f"sanV_{ee}"],), name=f"sanM_{ee}")
        def _(e, b):
            return e.activation(mski[:, :], mskf[:, :], AF.Identity,
                                bias=0.0, scale=1.0)
        tok[f"sanM_{ee}"] = bld.tk("scalar")

        @bld.step("vector", after=(tok[f"sanM_{ee}"],), name=f"sanP_{ee}")
        def _(e, b, _e=ee):
            return e.copy_predicated(csafe[:, :], mski[:, :], comp_all[:, _e, :])
        tok[f"sanP_{ee}"] = bld.tk("vector")

        @bld.step("scalar", after=(tok[f"sanP_{ee}"],), name=f"sanI_{ee}")
        def _(e, b, _e=ee):
            return e.activation(slot_idx[0:16, _e, :], csafe[:, :], AF.Identity,
                                bias=0.0, scale=1.0)
        tok[f"sanI_{ee}"] = bld.tk("scalar")

    tok["idxrep"] = bld.reserve("gld", 1)

    @bld.step("gpsimd", after=(tok[f"sanI_{E-1}"], tok["cbc"]), name="idxrep")
    def _(e, b):
        b.dma(e, "gld", slot_idx[16:32, :, :], slot_idx[0:16, :, :])
        return None

    if DBG:
        tok["dbgr"] = bld.reserve("dbg", 13)

        @bld.step("scalar", after=(tok["gfin"], tok["goffev"], tok["idxrep"],
                                   tok["gates"], tok["Ssum"], tok["rprep"]),
                  name="dbg_route")
        def _(e, b):
            b.dma(e, "dbg", dbg["logits"], logits_tm[:, :, :])
            b.dma(e, "dbg", dbg["route"][:, 0, :], v1c[:, :])
            b.dma(e, "dbg", dbg["route"][:, 1, :], v2c[:, :])
            b.dma(e, "dbg", dbg["route"][:, 2, :], i1c[:, :])
            b.dma(e, "dbg", dbg["route"][:, 3, :], i2c[:, :])
            b.dma(e, "dbg", dbg["route"][:, 4, :], g1c[:, :])
            b.dma(e, "dbg", dbg["route"][:, 5, :], g2c[:, :])
            b.dma(e, "dbg", dbg["counts"], counts_ag[0:1, :, 0:E])
            b.dma(e, "dbg", dbg["goff"], goff_row[:, :])
            b.dma(e, "dbg", dbg["S"], S_bf[:, :, :].rearrange("p a b -> p (a b)"))
            b.dma(e, "dbg", dbg["qks"], qks[:, :, :])
            b.dma(e, "dbg", dbg["gslot"], gslot[:, :])
            b.dma(e, "dbg", dbg["comp"], comp_all[:, :, :])
            return None
        tok["dbg_route"] = bld.tk("scalar")

    rctx.close()
    r_ps.close()
    mctx = ExitStack()

    def msb(name, shape, dtype):
        return mctx.enter_context(nc.sbuf_tensor(name, shape, dtype))

    P_rows = msb("P_rows", [128, DC, D], BF16)
    w1s = msb("w1s", [128, 2, 2, DC, 128], BF16)
    w2s = msb("w2s", [128, 2, NF + 1, 512], BF16)
    H_sb = msb("H_sb", [128, NF, CAP], BF16)
    O_sb = msb("O_sb", [128, NJ, D], BF16)
    m_ps = ExitStack()
    ps_h = [m_ps.enter_context(nc.psum_tensor(f"ps_h{i}", [128, CAP], F32))
            for i in range(2)]
    ps_o = [m_ps.enter_context(nc.psum_tensor(f"ps_o{i}", [128, 512], F32))
            for i in range(2)]
    ps_tp2 = m_ps.enter_context(nc.psum_tensor("ps_tp2", [128, 2, DC, 128],
                                               BF16))

    @bld.step("vector", after=(tok["gstore"], tok[f"kept_{E-1}"]),
              name="w2aug_zero")
    def _(e, b):
        return e.memset(w2s[:, :, NF, :], 0.0)
    tok["w2aug_zero"] = bld.tk("vector")

    last_x2st = tok[f"x2st_{NTL-1}_{CPT-1}"]
    last_x2st2 = tok[f"x2st_{NTL-1}_{CPT-2}"]

    for ee in range(E):
        tok[f"pg_{ee}"] = bld.reserve("moe", 1)

        @bld.step("gpsimd",
                  after=(tok["idxrep"], last_x2st, last_x2st2,
                         tok["zero_osl"], tok.get(f"Amm_{ee-1}_{NF-1}"),
                         (tok["gstore"] if ee == 0 else None),
                         (tok[f"kept_{E-1}"] if ee == 0 else None)),
                  name=f"pg_{ee}")
        def _(e, b, _e=ee):
            e.dma_gather(P_rows[:, :, :], xn2_hbm[0:2 * T + 128, :],
                         slot_idx[:, _e, :], CAP, CAP, D,
                         transpose=False).then_inc(b.seg_sem("moe"), 16)
            return None

        for c in range(DC):
            @bld.step("tensor",
                      after=(tok[f"pg_{ee}"], tok.get(f"tpe_{ee}_{c-2}"),
                             (tok.get(f"Amm_{ee-1}_{NF-1}") if c == 0 else None),
                             (tok["gstore"] if (ee == 0 and c == 0) else None),
                             (tok[f"kept_{E-1}"] if (ee == 0 and c == 0)
                              else None)),
                      name=f"tp_{ee}_{c}")
            def _(e, b, _c=c):
                last = None
                for dc in range(DC):
                    last = e.transpose(ps_tp2[:, _c % 2, dc, :],
                                       P_rows[:, _c, dc * 128:(dc + 1) * 128],
                                       ident_b[:, :])
                return last
            tok[f"tp_{ee}_{c}"] = bld.tk("tensor")

            @bld.step("vector", after=(tok[f"tp_{ee}_{c}"],
                                       (tok.get(f"Amm_{ee-1}_{NF-1}") if c == 0
                                        else None)),
                      name=f"tpe_{ee}_{c}")
            def _(e, b, _c=c):
                last = None
                for dc in range(DC):
                    last = e.tensor_copy(
                        P_rows[:, _c, dc * 128:(dc + 1) * 128],
                        ps_tp2[:, _c % 2, dc, :])
                return last
            tok[f"tpe_{ee}_{c}"] = bld.tk("vector")

        def decl_w2l(_ee, _h, war):
            tok[f"w2l_{_ee}_{_h}"] = bld.reserve("wld", 2)

            @bld.step("gpsimd", after=(tok["w2aug_zero"], war),
                      name=f"w2l_{_ee}_{_h}")
            def _(e, b, _e=_ee, _hh=_h):
                hb = (2 * _e + _hh) % 2
                b.dma(e, "wld", w2s[:, hb, 0:NF, :],
                      w2_d[_e, :, :, _hh * 512:(_hh + 1) * 512].rearrange(
                          "f p d -> p f d"))
                b.dma(e, "wld", w2s[0:1, hb, NF, :],
                      b2_d[_e:_e + 1, _hh * 512:(_hh + 1) * 512])
                return None

        decl_w2l(ee, 0, tok.get(f"Bmm_{ee-1}_0_{NJ-1}"))
        decl_w2l(ee, 1, tok.get(f"Bmm_{ee-1}_1_{NJ-1}"))

        def decl_w1l(_ee, _grp, w1war):
            tok[f"w1l_{_ee}_{_grp}"] = bld.reserve("wld", 1)

            @bld.step("gpsimd", after=(w1war,
                                       (tok["w2aug_zero"] if _ee == 0 and
                                        _grp < 2 else None)),
                      name=f"w1l_{_ee}_{_grp}")
            def _(e, b, _e=_ee, _g=_grp):
                b.dma(e, "wld", w1s[:, _g % 2, :, :, :],
                      w1_d[_e, 2 * _g:2 * _g + 2].rearrange(
                          "f p c m -> p f c m"))
                return None

        for grp in range(2):
            decl_w1l(ee, grp, tok.get(f"Amm_{ee-1}_{NF-4+2*grp+1}"))

        for f in range(NF):
            grp = f // 2
            if f >= 2:
                psh_war = tok.get(f"Aev_{ee}_{f-2}")
            else:
                psh_war = tok.get(f"Aev_{ee-1}_{NF-2+f}")

            @bld.step("tensor", after=(tok[f"w1l_{ee}_{grp}"],
                                       tok[f"tpe_{ee}_{DC-1}"],
                                       psh_war,
                                       (tok.get(f"Bmm_{ee-1}_1_{NJ-1}") if f == 0
                                        else None),
                                       (tok[f"kept_{E-1}"] if (ee == 0 and f == 0)
                                        else None),
                                       (tok["gstore"] if (ee == 0 and f == 0)
                                        else None)),
                      name=f"Amm_{ee}_{f}")
            def _(e, b, _e=ee, _f=f):
                gp = (_f // 2) % 2
                for dc in range(DC):
                    e.matmul(ps_h[_f % 2][:, 0:512], w1s[:, gp, _f % 2, dc, :],
                             P_rows[:, 0:4, dc * 128:(dc + 1) * 128],
                             start=(dc == 0), stop=False,
                             skip_group_check=True)
                    e.matmul(ps_h[_f % 2][:, 512:1024], w1s[:, gp, _f % 2, dc, :],
                             P_rows[:, 4:8, dc * 128:(dc + 1) * 128],
                             start=(dc == 0), stop=(dc == DC - 1),
                             skip_group_check=True)
                return None
            tok[f"Amm_{ee}_{f}"] = bld.tk("tensor")

            @bld.step("scalar", after=(tok[f"Amm_{ee}_{f}"],), name=f"Aev_{ee}_{f}")
            def _(e, b, _e=ee, _f=f):
                return e.activation(H_sb[:, _f, :], ps_h[_f % 2][:, :], AF.Gelu,
                                    bias=b1pp[:, _e, _f:_f + 1], scale=1.0)
            tok[f"Aev_{ee}_{f}"] = bld.tk("scalar")

            if f % 2 == 1 and f // 2 + 2 < NF // 2:
                decl_w1l(ee, f // 2 + 2, tok[f"Amm_{ee}_{f}"])

        for hh in range(2):
            for j in range(NJ):
                q = hh * NJ + j
                if q >= 2:
                    po_war = tok.get(f"Bev_{ee}_{(q-2)//NJ}_{(q-2)%NJ}")
                else:
                    po_war = tok.get(f"Bev_{ee-1}_{(2*NJ-2+q)//NJ}_"
                                     f"{(2*NJ-2+q)%NJ}")

                @bld.step("tensor", after=(tok[f"Aev_{ee}_{NF-1}"],
                                           tok[f"w2l_{ee}_{hh}"], po_war),
                          name=f"Bmm_{ee}_{hh}_{j}")
                def _(e, b, _e=ee, _h=hh, _j=j, _q=q):
                    jp = _q % 2
                    hb = _h % 2
                    for f in range(NF):
                        e.matmul(ps_o[jp][:, :],
                                 H_sb[:, f, _j * 128:(_j + 1) * 128],
                                 w2s[:, hb, f, :], start=(f == 0), stop=False,
                                 skip_group_check=True)
                    return e.matmul(ps_o[jp][:, :], onescol0[:, :],
                                    w2s[:, hb, NF, :], start=False, stop=True,
                                    skip_group_check=True)
                tok[f"Bmm_{ee}_{hh}_{j}"] = bld.tk("tensor")

                @bld.step("vector", after=(tok[f"Bmm_{ee}_{hh}_{j}"],
                                           tok.get(f"scat_{ee-1}")),
                          name=f"Bev_{ee}_{hh}_{j}")
                def _(e, b, _e=ee, _h=hh, _j=j, _q=q):
                    return e.tensor_copy(
                        O_sb[:, _j, _h * 512:(_h + 1) * 512], ps_o[_q % 2][:, :])
                tok[f"Bev_{ee}_{hh}_{j}"] = bld.tk("vector")

        tok[f"scat_{ee}"] = bld.reserve("moe", 1)

        @bld.step("gpsimd", after=(tok[f"Bev_{ee}_1_{NJ-1}"], tok["zero_osl"]),
                  name=f"scat_{ee}")
        def _(e, b, _e=ee):
            e.dma_scatter_add(osl_hbm[0:SLOTS + 128, :], O_sb[:, :, :],
                              slot_idx[:, _e, :], CAP, CAP, D).then_inc(
                                  b.seg_sem("moe"), 16)
            return None

    mctx.close()
    m_ps.close()
    cctx = ExitStack()
    G_sb = cctx.enter_context(nc.sbuf_tensor("G_sb", [128, 2, 2, D], BF16))
    xmc = cctx.enter_context(nc.sbuf_tensor("xmc", [128, 2, D], F32))
    outc = cctx.enter_context(nc.sbuf_tensor("outc", [128, 2, D], F32))

    for ch in range(NCH):
        par = ch % 2
        tok[f"gl_{ch}"] = bld.reserve("ld", 2)

        @bld.step("sync", after=(tok[f"scat_{E-1}"],
                                 (tok[f"scat_{E-2}"] if ch == 0 else None),
                                 (tok[f"xmst_{NTL-1}_{CPT-1}"] if ch == 0 else None),
                                 (tok[f"xmst_{NTL-1}_{CPT-2}"] if ch == 0 else None),
                                 tok.get(f"comb_{ch-2}")),
                  name=f"gl_{ch}")
        def _(e, b, _ch=ch, _p=par):
            b.dma(e, "ld", G_sb[:, _p, :, :],
                  osl_hbm[_ch * 256:(_ch + 1) * 256, :].rearrange(
                      "(p k) d -> p k d", k=2))
            b.dma(e, "ld", xmc[:, _p, :], xmid_hbm[_ch * 128:(_ch + 1) * 128, :])
            return None

        @bld.step("vector", after=(tok[f"gl_{ch}"], tok.get(f"ost_{ch-2}")),
                  name=f"comb_{ch}")
        def _(e, b, _ch=ch, _p=par):
            e.tensor_scalar(outc[:, _p, :], G_sb[:, _p, 0, :],
                            g0k[:, _ch:_ch + 1], None, ALU.mult)
            e.drain()
            e.scalar_tensor_tensor(outc[:, _p, :], G_sb[:, _p, 1, :],
                                   g1k[:, _ch:_ch + 1], outc[:, _p, :],
                                   ALU.mult, ALU.add)
            e.drain()
            return e.tensor_tensor(outc[:, _p, :], outc[:, _p, :], xmc[:, _p, :],
                                   ALU.add)
        tok[f"comb_{ch}"] = bld.tk("vector")

        tok[f"ost_{ch}"] = bld.reserve("ost", 1)

        @bld.step("scalar", after=(tok[f"comb_{ch}"],), name=f"ost_{ch}")
        def _(e, b, _ch=ch, _p=par):
            b.dma(e, "ost", y_out[_ch * 128:(_ch + 1) * 128, :], outc[:, _p, :])
            return None
        tok[f"ost_s{ch}"] = bld.tk("scalar")

    if DBG:
        tok["dbg2"] = bld.reserve("dbg", 3)

        @bld.step("scalar", after=(tok[f"ost_s{NCH-1}"],), name="dbg_tail")
        def _(e, b):
            b.dma(e, "dbg", dbg["xmid"], xmid_hbm[0:T, :])
            b.dma(e, "dbg", dbg["xn2"], xn2_hbm.rearrange(
                "(t k) d -> t k d", k=2)[0:T, 0, :])
            b.dma(e, "dbg", dbg["weff"], weff_hbm[:, :, :, :])
            return None
        tok["dbg_tail"] = bld.tk("scalar")

    @bld.step("gpsimd", after=(tok[f"ost_s{NCH-1}"],) +
              ((tok["dbg_tail"],) if DBG else ()), name="fin")
    def _(e, b):
        b.wait_all(e, "ost")
        if DBG:
            b.wait_all(e, "dbg")
        return e.nop()

    build_nc.marks = {}
    for eng in ENGS:
        for (fn, waits, name, gq) in bld.steps[eng]:
            build_nc.marks[name] = gq
    bld.emit()
    cctx.close()
    gctx.close()
    return nc


def _prep(inputs):
    f32 = np.float32
    ins = {k: np.asarray(v) for k, v in inputs.items()}
    x = np.ascontiguousarray(ins["L0_attn_x"].astype(f32).reshape(2 * 16384, D))

    def tile_lhsT(w):
        return np.ascontiguousarray(
            np.asarray(w).astype(f32).reshape(DC, 128, DC, 128)
            .transpose(0, 2, 1, 3))

    def rows8(w):
        return np.ascontiguousarray(np.asarray(w).astype(f32).reshape(DC, 128, D))

    def pp(bvec):
        return np.asarray(bvec).astype(f32).reshape(DC, 128).T

    common = {
        "wqT": tile_lhsT(np.asarray(ins["L0_attn_Wq"]).astype(f32).T),
        "wkT": tile_lhsT(np.asarray(ins["L0_attn_Wk"]).astype(f32).T),
        "wvT": tile_lhsT(np.asarray(ins["L0_attn_Wv"]).astype(f32).T),
        "wq2": rows8(ins["L0_attn_Wq2"]),
        "wk2": rows8(ins["L0_attn_Wk2"]),
        "wv2": rows8(ins["L0_attn_Wv2"]),
        "wo": rows8(ins["L0_attn_Wo"]),
        "bqkv": np.ascontiguousarray(np.concatenate(
            [pp(ins["L0_attn_bq"]), pp(ins["L0_attn_bk"]),
             pp(ins["L0_attn_bv"]), np.zeros((128, DC), f32)], axis=1)),
        "rows4": np.ascontiguousarray(np.stack(
            [np.asarray(ins["L0_attn_bq2"]).astype(f32),
             np.asarray(ins["L0_attn_bk2"]).astype(f32),
             np.asarray(ins["L0_attn_bv2"]).astype(f32),
             np.asarray(ins["L0_attn_bo"]).astype(f32)])),
        "gamma_row": np.ascontiguousarray(
            np.asarray(ins["L0_attn_gamma"]).astype(f32)[None, :]),
        "beta_row": np.ascontiguousarray(
            np.asarray(ins["L0_attn_beta"]).astype(f32)[None, :]),
        "wrt": np.ascontiguousarray(
            np.asarray(ins["L0_ffn_router_w"]).astype(f32).T
            .reshape(DC, 128, E).transpose(1, 0, 2)),
        "rb_col": np.ascontiguousarray(
            np.asarray(ins["L0_ffn_router_b"]).astype(f32)[:, None]),
        "w1t": np.ascontiguousarray(
            np.asarray(ins["L0_ffn_w1"]).astype(f32)
            .reshape(E, DC, 128, NF, 128).transpose(0, 3, 2, 1, 4)),
        "w2t": np.ascontiguousarray(
            np.asarray(ins["L0_ffn_w2"]).astype(f32).reshape(E, NF, 128, D)),
        "b1_pp": np.ascontiguousarray(
            np.asarray(ins["L0_ffn_b1"]).astype(f32)
            .reshape(E, NF, 128).transpose(2, 0, 1)),
        "b2_rows": np.ascontiguousarray(np.asarray(ins["L0_ffn_b2"]).astype(f32)),
    }
    in_maps = []
    for c in range(NC_):
        m = dict(common)
        m["x"] = np.ascontiguousarray(x[c * T:(c + 1) * T])
        ml = np.zeros((1, NC_), f32)
        ml[0, :c] = 1.0
        m["maskLT"] = ml
        m["rk4"] = np.array([[c % 4]], np.uint32)
        in_maps.append(m)
    return in_maps


_nc_cache = {}


def kernel(**inputs):
    in_maps = _prep(inputs)
    key = ("dbg" if DBG else "std", TRUNC)
    if key not in _nc_cache:
        nc = build_nc()
        lower_extended_insts(nc)
        _nc_cache[key] = nc
    nc = _nc_cache[key]
    res = run_bass_kernel_spmd(nc, in_maps, core_ids=list(range(NC_)))
    y = np.concatenate([res.results[c]["y"] for c in range(NC_)], axis=0)
    out = np.ascontiguousarray(y.reshape(2, 16384, D).astype(np.float32))
    if DBG:
        kernel._dbg = res.results
    return out


# version 04
# speedup vs baseline: 1.0127x; 1.0127x over previous
import numpy as np
import concourse.bass as bass
import concourse.mybir as mybir
from concourse.bass_utils import run_bass_kernel_spmd
from concourse.library_overlay import lower_extended_insts
from concourse import library_config as libcfg
from contextlib import ExitStack

F32 = mybir.dt.float32
BF16 = mybir.dt.bfloat16
I16 = mybir.dt.int16
I32 = mybir.dt.int32
U32 = mybir.dt.uint32
ALU = mybir.AluOpType
AF = mybir.ActivationFunctionType
AX = mybir.AxisListType
ENG = mybir.EngineType
ds = bass.ds

NC_ = 8
D = 1024
DC = 8
H = 16
T = 4096
NCH = 32
NTL = 8
TPT = 512
CPT = 4
E = 16
NF = 32
CAP = 1024
NJ = CAP // 128
CAPF = CAP // 16
CAPG = 5120
SLOTS = 2 * T
DUMMY = float(SLOTS)
EPS1 = 1e-5
EPSA = 1e-3
EPSQ = 1e-6
EPSG = 1e-9

ENGS = ("sync", "scalar", "vector", "tensor", "gpsimd")
HAZARD = {"vector": True, "scalar": True, "tensor": False, "gpsimd": False,
          "sync": False}

DBG = False
TRUNC = None


class Build:
    STREAM_K = {"ld": 4, "wld": 4, "moe": 4}
    DEFAULT_K = 2

    def __init__(self, nc, ctx):
        self.nc = nc
        self.ctx = ctx
        self.steps = {e: [] for e in ENGS}
        self.esem = {e: ctx.enter_context(nc.semaphore(f"es_{e}")) for e in ENGS}
        self.dsem = {}
        self.resv = {}
        self.rt = {}
        self.segidx = {}
        self.lasttok = {}
        self.emit_q = {}
        self.pending_guards = []

    def _ds(self, name, slot):
        key = (name, slot)
        if key not in self.dsem:
            self.dsem[key] = self.ctx.enter_context(
                self.nc.semaphore(f"d_{name}{slot}"))
            self.resv[key] = 0
            self.rt[key] = 0
        return self.dsem[key]

    def reserve(self, name, n=1):
        k = self.STREAM_K.get(name, self.DEFAULT_K)
        s = self.segidx.get(name, 0)
        self.segidx[name] = s + 1
        slot = s % k
        sem = self._ds(name, slot)
        guard = self.lasttok.get((name, slot))
        if guard is not None:
            self.pending_guards.append(guard)
        self.resv[(name, slot)] += 16 * n
        tokv = (sem, self.resv[(name, slot)])
        self.lasttok[(name, slot)] = tokv
        self.emit_q.setdefault(name, []).append([slot, n])
        return tokv

    def _emit_slot(self, name):
        q = self.emit_q[name]
        assert q and q[0][1] > 0, f"stream {name}: dma beyond reservation"
        slot = q[0][0]
        q[0][1] -= 1
        if q[0][1] == 0:
            q.pop(0)
        self.rt[(name, slot)] += 16
        return self._ds(name, slot)

    def dma(self, e, name, out, in_, **kw):
        s = self._emit_slot(name)
        e.dma_start(out, in_, **kw).then_inc(s, 16)

    def seg_sem(self, name):
        return self._emit_slot(name)

    def wait_all(self, e, name):
        k = self.STREAM_K.get(name, self.DEFAULT_K)
        for slot in range(k):
            if (name, slot) in self.resv and self.resv[(name, slot)]:
                e.wait_ge(self.dsem[(name, slot)], self.resv[(name, slot)])

    gseq = 0

    def step(self, eng, after=(), name=""):
        def deco(fn):
            waits = [t for t in after if t is not None] + self.pending_guards
            self.pending_guards = []
            Build.gseq += 1
            self.steps[eng].append((fn, waits, name, Build.gseq))
            return (self.esem[eng], len(self.steps[eng]))
        return deco

    def tk(self, eng):
        return (self.esem[eng], len(self.steps[eng]))

    def emit(self):
        nc = self.nc
        with nc.Block() as blk:
            for engname in ENGS:
                steps = self.steps[engname]
                if not steps:
                    continue

                def body(e, _steps=steps, _eng=engname):
                    waited = {}
                    own = self.esem[_eng]
                    for (fn, waits, name, gq) in _steps:
                        if TRUNC is not None and gq > TRUNC:
                            continue
                        for (sem, val) in waits:
                            if sem is own and not HAZARD[_eng]:
                                continue
                            if waited.get(id(sem), -1) >= val:
                                continue
                            e.wait_ge(sem, val)
                            waited[id(sem)] = val
                        fn(e, self)
                        e.drain().then_inc(own, 1)

                getattr(blk, engname)(body)
        if TRUNC is None:
            for k in self.resv:
                assert self.resv[k] == self.rt[k], \
                    f"dma stream {k}: reserved {self.resv[k]} != emitted {self.rt[k]}"
            assert not self.pending_guards, "unconsumed dma segment guards"


def build_nc():
    nc = bass.Bass("TRN2", target_bir_lowering=False)

    x_in = nc.dram_tensor("x", [T, D], F32, kind="ExternalInput").ap()
    wqT_d = nc.dram_tensor("wqT", [DC, DC, 128, 128], F32, kind="ExternalInput").ap()
    wkT_d = nc.dram_tensor("wkT", [DC, DC, 128, 128], F32, kind="ExternalInput").ap()
    wvT_d = nc.dram_tensor("wvT", [DC, DC, 128, 128], F32, kind="ExternalInput").ap()
    wq2_d = nc.dram_tensor("wq2", [DC, 128, D], F32, kind="ExternalInput").ap()
    wk2_d = nc.dram_tensor("wk2", [DC, 128, D], F32, kind="ExternalInput").ap()
    wv2_d = nc.dram_tensor("wv2", [DC, 128, D], F32, kind="ExternalInput").ap()
    wo_d = nc.dram_tensor("wo", [DC, 128, D], F32, kind="ExternalInput").ap()
    bqkv_d = nc.dram_tensor("bqkv", [128, 4 * DC], F32, kind="ExternalInput").ap()
    rows_d = nc.dram_tensor("rows4", [4, D], F32, kind="ExternalInput").ap()
    gamma_d = nc.dram_tensor("gamma_row", [1, D], F32, kind="ExternalInput").ap()
    beta_d = nc.dram_tensor("beta_row", [1, D], F32, kind="ExternalInput").ap()
    wrt_d = nc.dram_tensor("wrt", [128, DC, E], F32, kind="ExternalInput").ap()
    rb_d = nc.dram_tensor("rb_col", [E, 1], F32, kind="ExternalInput").ap()
    w1_d = nc.dram_tensor("w1t", [E, NF, 128, DC, 128], F32, kind="ExternalInput").ap()
    w2_d = nc.dram_tensor("w2t", [E, NF, 128, D], F32, kind="ExternalInput").ap()
    b1_d = nc.dram_tensor("b1_pp", [128, E, NF], F32, kind="ExternalInput").ap()
    b2_d = nc.dram_tensor("b2_rows", [E, D], F32, kind="ExternalInput").ap()
    maskLT_d = nc.dram_tensor("maskLT", [1, NC_], F32, kind="ExternalInput").ap()
    rk4_d = nc.dram_tensor("rk4", [1, 1], U32, kind="ExternalInput").ap()

    y_out = nc.dram_tensor("y", [T, D], F32, kind="ExternalOutput").ap()
    dbg = {}
    if DBG:
        for nm, shp, dt in [("xmid", [T, D], F32), ("logits", [128, NCH, H], F32),
                            ("route", [128, 6, NCH], F32), ("counts", [1, NC_, E], F32),
                            ("goff", [1, E], F32), ("S", [128, DC * 128], BF16),
                            ("qks", [128, NCH, H], F32), ("gslot", [32, 256], F32),
                            ("comp", [16, E, CAPF], F32), ("xn2", [T, D], BF16),
                            ("weff", [3, 128, DC, D], BF16)]:
            dbg[nm] = nc.dram_tensor("dbg_" + nm, shp, dt, kind="ExternalOutput").ap()

    xn1_hbm = nc.dram_tensor("xn1_hbm", [DC, 128, T], BF16).ap()
    bvp_hbm = nc.dram_tensor("bvp_hbm", [1, D], F32).ap()
    weff_hbm = nc.dram_tensor("weff_hbm", [3, 128, DC, D], BF16).ap()
    xmid_hbm = nc.dram_tensor("xmid_hbm", [T, D], F32).ap()
    xn2_hbm = nc.dram_tensor("xn2_hbm", [2 * T + 128, D], BF16).ap()
    osl_hbm = nc.dram_tensor("oslots_hbm", [SLOTS + 128, D], BF16).ap()

    pid_ap = nc.partition_id_tensor.ap()

    gctx = ExitStack()
    Build.gseq = 0
    bld = Build(nc, gctx)

    def sb(name, shape, dtype):
        return gctx.enter_context(nc.sbuf_tensor(name, shape, dtype))

    rsem = gctx.enter_context(nc.semaphore("rsem"))
    lsem = gctx.enter_context(nc.semaphore("lsem"))
    rsem2 = gctx.enter_context(nc.semaphore("rsem2"))
    lsem2 = gctx.enter_context(nc.semaphore("lsem2"))
    rsemS2 = gctx.enter_context(nc.semaphore("rsemS2"))
    lsemS2 = gctx.enter_context(nc.semaphore("lsemS2"))
    psem = gctx.enter_context(nc.semaphore("psem"))

    stats = sb("stats", [128, 8, NCH], F32)
    qks = sb("qks", [128, NCH, H], F32)
    r_tm = sb("r_tm", [128, NCH, H], F32)
    S_bf = sb("S_bf", [128, DC, 128], BF16)
    logits_tm = sb("logits_tm", [128, NCH, H], F32)
    onesb = sb("onesb", [1, 128], BF16)
    onesf = sb("onesf", [1, 128], F32)
    onescol0 = sb("onescol0", [128, 128], BF16)
    beffv_pp = sb("beffv_pp", [128, DC], F32)
    biaspp = sb("biaspp", [128, 4 * DC], F32)
    biasppb = sb("biasppb", [128, 4 * DC], BF16)
    b1pp = sb("b1pp", [128, E, NF], F32)
    wrt_sb = sb("wrt_sb", [128, DC, E], F32)
    rb_sb = sb("rb_sb", [16, 1], F32)
    maskLT_sb = sb("maskLT_sb", [1, NC_], F32)
    rk_sb = sb("rk_sb", [1, 1], U32)
    rk4_sb = sb("rk4_sb", [1, 1], U32)
    iot16 = sb("iot16", [128, H], F32)
    iotB = sb("iotB", [128, H], F32)
    ident_b = sb("ident_b", [128, 128], BF16)
    ident_f = sb("ident_f", [128, 128], F32)
    tri32 = sb("tri32", [32, 32], F32)
    selp = sb("selp", [16, DC, 128], BF16)
    v1c = sb("v1c", [128, NCH], F32)
    v2c = sb("v2c", [128, NCH], F32)
    i1c = sb("i1c", [128, NCH], F32)
    i2c = sb("i2c", [128, NCH], F32)
    g1c = sb("g1c", [128, NCH], F32)
    g2c = sb("g2c", [128, NCH], F32)
    sc16 = sb("sc16", [128, H], F32)
    sc16b = sb("sc16b", [128, H], F32)
    stt1 = sb("stt1", [128, NCH], F32)
    stt2 = sb("stt2", [128, NCH], F32)
    i1T = sb("i1T", [32, 128], F32)
    i2T = sb("i2T", [32, 128], F32)
    g1T = sb("g1T", [32, 128], F32)
    g2T = sb("g2T", [32, 128], F32)
    zer32 = sb("zer32", [32, 256], F32)
    siota = sb("siota", [32, 256], F32)
    comp_all = sb("comp_all", [16, E, CAPF], F32)
    spill_pad = sb("spill_pad", [16, 256], F32)
    counts_u = sb("counts_u", [1, E], U32)
    counts_fx = sb("counts_fx", [128, 64], F32)
    counts_bc = sb("counts_bc", [16, E], F32)
    counts_ag = sb("counts_ag", [128, NC_, 64], F32)
    goff_row = sb("goff_row", [1, E], F32)
    wiota = sb("wiota", [16, CAPF], F32)
    mskf = sb("mskf", [16, CAPF], F32)
    mski = sb("mski", [16, CAPF], I32)
    csafe = sb("csafe", [16, CAPF], F32)
    slot_idx = sb("slot_idx", [128, E, CAPF], I16)
    gslot = sb("gslot", [32, 256], F32)
    kacc = sb("kacc", [32, 256], F32)
    g0k = sb("g0k", [128, NCH], F32)
    g1k = sb("g1k", [128, NCH], F32)

    tok = {}

    pctx = ExitStack()
    gam_b = pctx.enter_context(nc.sbuf_tensor("gam_b", [128, D], F32))
    bet_b = pctx.enter_context(nc.sbuf_tensor("bet_b", [128, D], F32))
    beff_rows = pctx.enter_context(nc.sbuf_tensor("beff_rows", [1, 4, D], BF16))

    tok["setup_ld"] = bld.reserve("ld", 7)

    @bld.step("sync", name="setup_loads")
    def _(e, b):
        b.dma(e, "ld", biaspp[:, :], bqkv_d)
        b.dma(e, "ld", b1pp[:, :, :], b1_d)
        b.dma(e, "ld", wrt_sb[:, :, :], wrt_d)
        b.dma(e, "ld", maskLT_sb[:, :], maskLT_d)
        b.dma(e, "ld", rk_sb[:, :], pid_ap)
        b.dma(e, "ld", rk4_sb[:, :], rk4_d)
        b.dma(e, "ld", rb_sb[:, :], rb_d)
        return None

    @bld.step("gpsimd", name="setup_gp")
    def _(e, b):
        e.iota(iot16[:, :], [[1, H]], base=0, channel_multiplier=0,
               allow_small_or_imprecise_dtypes=True)
        e.iota(iotB[:, :], [[1, H]], base=2 ** 20, channel_multiplier=0,
               allow_small_or_imprecise_dtypes=True)
        e.iota(ident_f[:, :], [[1, 128]], base=0, channel_multiplier=-1,
               allow_small_or_imprecise_dtypes=True)
        e.drain()
        e.tensor_single_scalar(ident_f[:, :], ident_f[:, :], 0.0, ALU.is_equal)
        e.drain()
        e.tensor_copy(ident_b[:, :], ident_f[:, :])
        e.iota(tri32[:, :], [[1, 32]], base=0, channel_multiplier=-1,
               allow_small_or_imprecise_dtypes=True)
        e.drain()
        e.tensor_single_scalar(tri32[:, :], tri32[:, :], 0.0, ALU.is_gt)
        e.iota(siota[:, :], [[1, 256]], base=0, channel_multiplier=256,
               allow_small_or_imprecise_dtypes=True)
        e.iota(wiota[:, :], [[16, CAPF]], base=0, channel_multiplier=1,
               allow_small_or_imprecise_dtypes=True)
        e.iota(selp[:, :, :], [[0, DC], [1, 2], [0, 64]], base=0,
               channel_multiplier=-1, allow_small_or_imprecise_dtypes=True)
        e.drain()
        for p in range(DC):
            e.tensor_single_scalar(selp[:, p, :], selp[:, p, :], float(-2 * p),
                                   ALU.is_equal)
        e.memset(zer32[:, :], 0.0)
        e.memset(onesf[:, :], 1.0)
        e.drain()
        e.tensor_copy(onesb[:, :], onesf[:, :])
        e.memset(onescol0[:, :], 0.0)
        e.drain()
        e.memset(onescol0[0:1, :], 1.0)
        e.memset(slot_idx[:, :, :], 0)
        return e.nop()
    tok["setup_gp0"] = bld.tk("gpsimd")

    @bld.step("gpsimd", after=(tok["setup_ld"],), name="biasppb_cvt")
    def _(e, b):
        return e.tensor_copy(biasppb[:, :], biaspp[:, :])
    tok["setup_gp"] = bld.tk("gpsimd")
    tok["biasppb"] = tok["setup_gp"]

    sctx = ExitStack()
    gb_rows = sctx.enter_context(nc.sbuf_tensor("gb_rows", [1, 2, D], F32))
    bvrow_f = sctx.enter_context(nc.sbuf_tensor("bvrow_f", [1, D], F32))
    zrow = sctx.enter_context(nc.sbuf_tensor("zrow", [128, D], BF16))
    zrowf = sctx.enter_context(nc.sbuf_tensor("zrowf", [128, 64], F32))
    rows_raw = sctx.enter_context(nc.sbuf_tensor("rows_raw", [1, 4, D], BF16))
    s_ps = ExitStack()
    ps_set = s_ps.enter_context(nc.psum_tensor("ps_set", [128, 512], F32))

    tok["gbld"] = bld.reserve("ld", 2)

    @bld.step("sync", name="gbld")
    def _(e, b):
        b.dma(e, "ld", gb_rows[0:1, 0, :], gamma_d)
        b.dma(e, "ld", gb_rows[0:1, 1, :], beta_d)
        return None

    tok["rows_raw"] = bld.reserve("gld", 1)

    @bld.step("gpsimd", name="rows_raw_ld")
    def _(e, b):
        b.dma(e, "gld", rows_raw[0:1, :, :], rows_d)
        return None

    prev_ps_set = None
    for gi in range(4):
        dst = gam_b if gi < 2 else bet_b
        row = 0 if gi < 2 else 1
        half = gi % 2

        @bld.step("tensor", after=(tok["gbld"], tok["setup_gp"], prev_ps_set),
                  name=f"gbmm{gi}")
        def _(e, b, _r=row, _h=half):
            return e.matmul(ps_set[:, :], onesf[0:1, :],
                            gb_rows[0:1, _r, _h * 512:(_h + 1) * 512],
                            start=True, stop=True)
        tok[f"gbmm{gi}"] = bld.tk("tensor")

        @bld.step("vector", after=(tok[f"gbmm{gi}"],), name=f"gbev{gi}")
        def _(e, b, _d=dst, _h=half):
            return e.tensor_copy(_d[:, _h * 512:(_h + 1) * 512], ps_set[:, :])
        tok[f"gbev{gi}"] = bld.tk("vector")
        prev_ps_set = tok[f"gbev{gi}"]
    tok["gb_done"] = tok["gbev3"]

    @bld.step("vector", name="zrow")
    def _(e, b):
        e.memset(zrow[:, :], 0.0)
        e.drain()
        return e.memset(zrowf[:, :], 0.0)
    tok["zrow"] = bld.tk("vector")

    tok["zero_osl"] = bld.reserve("zst", (SLOTS // 128 + 1) + 1)

    @bld.step("scalar", after=(tok["zrow"],), name="zero_osl")
    def _(e, b):
        for cc in range(SLOTS // 128 + 1):
            b.dma(e, "zst", osl_hbm[cc * 128:(cc + 1) * 128, :], zrow[:, :])
        b.dma(e, "zst", xn2_hbm[2 * T:2 * T + 128, :], zrow[:, :])
        return None

    fctx = ExitStack()
    wT_sb = fctx.enter_context(nc.sbuf_tensor("wT_sb", [128, DC * DC, 128], BF16))
    w2p_sb = fctx.enter_context(nc.sbuf_tensor("w2p_sb", [128, DC, D], BF16))
    stage = fctx.enter_context(nc.sbuf_tensor("stage", [128, 2, D], BF16))

    for wi, (wT_d_, w2_d_, bcol) in enumerate(
            [(wqT_d, wq2_d, 0), (wkT_d, wk2_d, DC), (wvT_d, wv2_d, 2 * DC)]):
        tok[f"fld_{wi}"] = bld.reserve("wld", 2)

        @bld.step("gpsimd", after=(tok.get(f"fdone_{wi-1}"),), name=f"fld_{wi}")
        def _(e, b, _wT=wT_d_, _w2=w2_d_):
            b.dma(e, "wld", wT_sb.ap().rearrange("p (m i) f -> p m i f", i=DC),
                  _wT.rearrange("m i p f -> p m i f"))
            b.dma(e, "wld", w2p_sb[:, :, :], _w2.rearrange("c p d -> p c d"))
            return None

        prev_fev = prev_ps_set
        for ic in range(DC):
            for hf in range(2):
                @bld.step("tensor", after=(tok[f"fld_{wi}"], prev_fev),
                          name=f"fmm_{wi}_{ic}_{hf}")
                def _(e, b, _ic=ic, _hf=hf):
                    last = None
                    sl = slice(_hf * 512, (_hf + 1) * 512)
                    for mc in range(DC):
                        last = e.matmul(ps_set[:, :], wT_sb[:, mc * DC + _ic, :],
                                        w2p_sb[:, mc, sl],
                                        start=(mc == 0), stop=(mc == DC - 1))
                    return last
                tok[f"fmm_{wi}_{ic}_{hf}"] = bld.tk("tensor")

                @bld.step("vector", after=(tok[f"fmm_{wi}_{ic}_{hf}"],
                                           tok.get(f"fstd_{wi}_{ic-2}"),
                                           (tok.get(f"fstd_{wi-1}_{DC-2+ic}")
                                            if ic < 2 else None)),
                          name=f"fev_{wi}_{ic}_{hf}")
                def _(e, b, _ic=ic, _hf=hf):
                    return e.tensor_copy(stage[:, _ic % 2, _hf * 512:(_hf + 1) * 512],
                                         ps_set[:, :])
                tok[f"fev_{wi}_{ic}_{hf}"] = bld.tk("vector")
                prev_fev = tok[f"fev_{wi}_{ic}_{hf}"]

            tok[f"fstd_{wi}_{ic}"] = bld.reserve("xst", 1)

            @bld.step("sync", after=(tok[f"fev_{wi}_{ic}_1"],),
                      name=f"fst_{wi}_{ic}")
            def _(e, b, _wi=wi, _ic=ic):
                b.dma(e, "xst", weff_hbm[_wi, :, _ic, :], stage[:, _ic % 2, :])
                return None

        for hf in range(2):
            @bld.step("tensor", after=(tok[f"fld_{wi}"], tok["biasppb"],
                                       tok["rows_raw"], prev_fev,
                                       tok.get(f"bfev_{wi}_{hf-1}")),
                      name=f"bfmm_{wi}_{hf}")
            def _(e, b, _bc=bcol, _wi=wi, _hf=hf):
                sl = slice(_hf * 512, (_hf + 1) * 512)
                last = None
                for mc in range(DC):
                    last = e.matmul(ps_set[0:1, :],
                                    biasppb[:, _bc + mc:_bc + mc + 1],
                                    w2p_sb[:, mc, sl], start=(mc == 0),
                                    stop=(mc == DC - 1))
                return last
            tok[f"bfmm_{wi}_{hf}"] = bld.tk("tensor")

            @bld.step("vector", after=(tok[f"bfmm_{wi}_{hf}"],),
                      name=f"bfev_{wi}_{hf}")
            def _(e, b, _wi=wi, _hf=hf):
                sl = slice(_hf * 512, (_hf + 1) * 512)
                e.tensor_tensor(beff_rows[0:1, _wi, sl], ps_set[0:1, :],
                                rows_raw[0:1, _wi, sl], ALU.add)
                if _wi == 2:
                    e.drain()
                    return e.tensor_tensor(bvrow_f[0:1, sl], ps_set[0:1, :],
                                           rows_raw[0:1, _wi, sl], ALU.add)
                return None
            tok[f"bfev_{wi}_{hf}"] = bld.tk("vector")
        tok[f"fdone_{wi}"] = tok[f"bfev_{wi}_1"]
        prev_ps_set = tok[f"bfev_{wi}_1"]

    @bld.step("vector", after=(tok["rows_raw"], tok["fdone_2"]), name="bo_copy")
    def _(e, b):
        return e.tensor_copy(beff_rows[0:1, 3, :], rows_raw[0:1, 3, :])
    tok["bo_copy"] = bld.tk("vector")

    tok["bvst"] = bld.reserve("xst", 1)

    @bld.step("sync", after=(tok["fdone_2"], tok["bo_copy"]), name="bvst")
    def _(e, b):
        b.dma(e, "xst", bvp_hbm[0:1, :], bvrow_f[0:1, :])
        return None

    tok["bvld"] = bld.reserve("ld", 1)

    @bld.step("sync", after=(tok["bvst"],), name="bvld")
    def _(e, b):
        b.dma(e, "ld", beffv_pp[:, :],
              bvp_hbm.rearrange("o (c p) -> (o p) c", p=128),
              allow_slow_non_contiguous=True)
        return None
    tok["beffv_done"] = tok["bvld"]

    fctx.close()
    s_ps.close()
    sctx.close()

    actx = ExitStack()

    def asb(name, shape, dtype):
        return actx.enter_context(nc.sbuf_tensor(name, shape, dtype))

    x_t = asb("x_t", [128, CPT, D], F32)
    xnb = asb("xnb", [128, 2, D], BF16)
    xnfm = asb("xnfm", [128, DC, TPT], BF16)
    q2p = asb("q2p", [128, 2, D], BF16)
    k2p = asb("k2p", [128, 2, D], BF16)
    tmin = asb("tmin", [128, D], F32)
    texp = asb("texp", [128, D], F32)
    sjunk = asb("sjunk", [128, D], F32)
    ybuf = asb("ybuf", [128, CPT, D], F32)
    zbuf = asb("zbuf", [128, D], F32)
    vtmp = asb("vtmp", [128, D], F32)
    x2fm = asb("x2fm", [128, DC, 128], F32)
    lgfm = asb("lgfm", [16, 128], F32)
    v2fm = asb("v2fm", [128, DC, TPT], BF16)
    v2r = asb("v2r", [128, 2, TPT], BF16)
    outsb = asb("outsb", [128, DC, TPT], BF16)
    r_fm = asb("r_fm", [16, T], BF16)
    agS = asb("agS", [128, 4, DC * 128], BF16)
    S_loc = asb("S_loc", [128, DC * 128], F32)
    S_snd = asb("S_snd", [128, DC * 128], BF16)

    a_ps = ExitStack()
    ps_qv = a_ps.enter_context(nc.psum_tensor("ps_qv", [128, 512], F32))
    ps_q2 = a_ps.enter_context(nc.psum_tensor("ps_q2", [128, 1024], F32))
    p1_ps = ExitStack()
    ps_tp = p1_ps.enter_context(nc.psum_tensor("ps_tp", [128, 2, 4, 128], BF16))
    ps_S = p1_ps.enter_context(nc.psum_tensor("ps_S", [128, DC, 128], F32))

    w1ctx = ExitStack()
    weffq = w1ctx.enter_context(nc.sbuf_tensor("weffq", [128, DC, D], BF16))
    weffk = w1ctx.enter_context(nc.sbuf_tensor("weffk", [128, DC, D], BF16))

    tok["p1_wload"] = bld.reserve("ld", 2 * DC)

    @bld.step("sync", after=(tok[f"fstd_0_{DC-1}"], tok[f"fstd_0_{DC-2}"],
                             tok[f"fstd_1_{DC-1}"], tok[f"fstd_1_{DC-2}"],
                             tok["beffv_done"], tok[f"fstd_2_{DC-1}"],
                             tok[f"fstd_2_{DC-2}"]),
              name="p1_wload")
    def _(e, b):
        for fi in range(DC):
            b.dma(e, "ld", weffq[:, fi, :], weff_hbm[0, :, fi, :])
            b.dma(e, "ld", weffk[:, fi, :], weff_hbm[1, :, fi, :])
        return None

    for t in range(NTL):
        tok[f"xload_{t}"] = bld.reserve("ld", CPT)

        @bld.step("sync", after=(tok.get(f"xn_{t-1}_{CPT-1}"),
                                 (tok["beffv_done"] if t == 0 else None),
                                 (tok[f"fstd_2_{DC-1}"] if t == 0 else None),
                                 (tok[f"fstd_2_{DC-2}"] if t == 0 else None),
                                 (tok["zero_osl"] if t == 0 else None)),
                  name=f"xload_{t}")
        def _(e, b, _t=t):
            for c in range(CPT):
                b.dma(e, "ld", x_t[:, c, :],
                      x_in[(_t * CPT + c) * 128:(_t * CPT + c + 1) * 128, :])
            return None

        @bld.step("scalar", after=(tok[f"xload_{t}"], tok.get(f"var1_{t-1}")),
                  name=f"st1_{t}")
        def _(e, b, _t=t):
            last = None
            for c in range(CPT):
                ch = _t * CPT + c
                e.activation(sjunk[:, :], x_t[:, c, :], AF.Identity,
                             bias=0.0, scale=1.0, accum_out=stats[:, 6, ch:ch + 1])
                e.drain()
                last = e.activation(sjunk[:, :], x_t[:, c, :], AF.Square,
                                    bias=0.0, scale=1.0,
                                    accum_out=stats[:, 7, ch:ch + 1])
                e.drain()
            return last
        tok[f"st1_{t}"] = bld.tk("scalar")

        @bld.step("vector", after=(tok[f"st1_{t}"],), name=f"var1_{t}")
        def _(e, b, _t=t):
            sl = slice(_t * CPT, (_t + 1) * CPT)
            e.tensor_scalar(stats[:, 0, sl], stats[:, 6, sl], 1.0 / D, None, ALU.mult)
            e.drain()
            e.tensor_scalar(stt1[:, 0:CPT], stats[:, 7, sl], 1.0 / D, None, ALU.mult)
            e.drain()
            e.tensor_tensor(stt2[:, 0:CPT], stats[:, 0, sl], stats[:, 0, sl],
                            ALU.mult)
            e.drain()
            e.tensor_tensor(stt1[:, 0:CPT], stt1[:, 0:CPT], stt2[:, 0:CPT],
                            ALU.subtract)
            e.drain()
            return e.tensor_scalar(stt1[:, 0:CPT], stt1[:, 0:CPT], 1.0, EPS1,
                                   ALU.mult, ALU.add)
        tok[f"var1_{t}"] = bld.tk("vector")

        @bld.step("scalar", after=(tok[f"var1_{t}"],), name=f"sq1_{t}")
        def _(e, b):
            return e.activation(stt2[:, 0:CPT], stt1[:, 0:CPT], AF.Sqrt,
                                bias=0.0, scale=1.0)
        tok[f"sq1_{t}"] = bld.tk("scalar")

        @bld.step("vector", after=(tok[f"sq1_{t}"],), name=f"rs1_{t}")
        def _(e, b, _t=t):
            return e.reciprocal(stats[:, 1, _t * CPT:(_t + 1) * CPT], stt2[:, 0:CPT])
        tok[f"rs1_{t}"] = bld.tk("vector")

        for c in range(CPT):
            gch = t * CPT + c

            @bld.step("vector", after=(tok[f"rs1_{t}"],
                                       tok.get(f"tpb_g{gch-2}")),
                      name=f"xn_{t}_{c}")
            def _(e, b, _t=t, _c=c):
                ch = _t * CPT + _c
                return e.tensor_scalar(xnb[:, _c % 2, :], x_t[:, _c, :],
                                       stats[:, 0, ch:ch + 1],
                                       stats[:, 1, ch:ch + 1],
                                       ALU.subtract, ALU.mult)
            tok[f"xn_{t}_{c}"] = bld.tk("vector")

            @bld.step("tensor", after=(tok[f"xn_{t}_{c}"], tok["setup_gp"],
                                       tok.get(f"tpeb_g{gch-2}"),
                                       ),
                      name=f"tpa_{t}_{c}")
            def _(e, b, _c=c):
                last = None
                for k in range(4):
                    last = e.transpose(ps_tp[:, _c % 2, k, :],
                                       xnb[:, _c % 2, k * 128:(k + 1) * 128],
                                       ident_b[:, :])
                return last
            tok[f"tpa_{t}_{c}"] = bld.tk("tensor")

            @bld.step("vector", after=(tok[f"tpa_{t}_{c}"],
                                       tok.get(f"q2mm_{t-1}_{c}_1"),
                                       (tok.get(f"xnst_{t-1}") if c == 0 else None)),
                      name=f"tpea_{t}_{c}")
            def _(e, b, _t=t, _c=c):
                dst = xnfm[:, 0:4, _c * 128:(_c + 1) * 128]
                return e.tensor_copy(dst, ps_tp[:, _c % 2, :, :])
            tok[f"tpea_{t}_{c}"] = bld.tk("vector")

            @bld.step("tensor", after=(tok[f"tpea_{t}_{c}"],), name=f"tpb_{t}_{c}")
            def _(e, b, _c=c):
                last = None
                for k in range(4):
                    last = e.transpose(ps_tp[:, _c % 2, k, :],
                                       xnb[:, _c % 2, (4 + k) * 128:(5 + k) * 128],
                                       ident_b[:, :])
                return last
            tok[f"tpb_{t}_{c}"] = bld.tk("tensor")
            tok[f"tpb_g{gch}"] = tok[f"tpb_{t}_{c}"]

            @bld.step("vector", after=(tok[f"tpb_{t}_{c}"],), name=f"tpeb_{t}_{c}")
            def _(e, b, _t=t, _c=c):
                dst = xnfm[:, 4:8, _c * 128:(_c + 1) * 128]
                return e.tensor_copy(dst, ps_tp[:, _c % 2, :, :])
            tok[f"tpeb_{t}_{c}"] = bld.tk("vector")
            tok[f"tpeb_g{gch}"] = tok[f"tpeb_{t}_{c}"]

            for qk in range(2):
                weff = weffq if qk == 0 else weffk
                dst2 = q2p if qk == 0 else k2p

                @bld.step("tensor",
                          after=(tok[f"tpeb_{t}_{c}"], tok[f"fdone_{qk}"],
                                 tok["p1_wload"],
                                 tok.get(f"elu2_g{gch}_{qk-1}"),
                                 (tok.get(f"elu2_g{gch-1}_1") if qk == 0 else None),
                                 tok.get(f"Smm_g{gch-2}"),
                                 tok.get(f"qks_g{gch-2}")),
                          name=f"q2mm_{t}_{c}_{qk}")
                def _(e, b, _t=t, _c=c, _ri=qk, _w=weff):
                    last = None
                    for half in range(2):
                        sl = slice(half * 512, (half + 1) * 512)
                        for fi in range(DC):
                            e.matmul(ps_q2[:, sl],
                                     xnfm[:, fi, _c * 128:(_c + 1) * 128],
                                     _w[:, fi, sl], start=(fi == 0), stop=False)
                        last = e.matmul(ps_q2[:, sl], onesb[0:1, :],
                                        beff_rows[0:1, _ri, sl], start=False,
                                        stop=True)
                    return last
                tok[f"q2mm_{t}_{c}_{qk}"] = bld.tk("tensor")
                tok[f"q2mm_g{gch}_{qk}"] = tok[f"q2mm_{t}_{c}_{qk}"]

                @bld.step("vector", after=(tok[f"q2mm_{t}_{c}_{qk}"],),
                          name=f"elu0_{t}_{c}_{qk}")
                def _(e, b):
                    return e.tensor_single_scalar(tmin[:, :], ps_q2[:, :], 0.0,
                                                  ALU.min)
                tok[f"elu0_{t}_{c}_{qk}"] = bld.tk("vector")

                @bld.step("scalar", after=(tok[f"elu0_{t}_{c}_{qk}"],),
                          name=f"elu1_{t}_{c}_{qk}")
                def _(e, b):
                    return e.activation(texp[:, :], tmin[:, :], AF.Exp,
                                        bias=0.0, scale=1.0)
                tok[f"elu1_{t}_{c}_{qk}"] = bld.tk("scalar")

                @bld.step("vector", after=(tok[f"elu1_{t}_{c}_{qk}"],),
                          name=f"elu2_{t}_{c}_{qk}")
                def _(e, b, _dst=dst2, _c=c):
                    return e.scalar_tensor_tensor(_dst[:, _c % 2, :], ps_q2[:, :],
                                                  0.0, texp[:, :], ALU.max, ALU.add)
                tok[f"elu2_{t}_{c}_{qk}"] = bld.tk("vector")
                tok[f"elu2_g{gch}_{qk}"] = tok[f"elu2_{t}_{c}_{qk}"]

            @bld.step("vector", after=(tok[f"elu2_g{gch}_0"], tok[f"elu2_g{gch}_1"]),
                      name=f"qks_{t}_{c}")
            def _(e, b, _t=t, _c=c):
                ch = _t * CPT + _c
                e.tensor_tensor(vtmp[:, :], q2p[:, _c % 2, :], k2p[:, _c % 2, :],
                                ALU.mult)
                e.drain()
                return e.tensor_reduce(qks[:, ch, :],
                                       vtmp.rearrange("p (h d) -> p h d", d=64),
                                       AX.X, ALU.add)
            tok[f"qks_{t}_{c}"] = bld.tk("vector")
            tok[f"qks_g{gch}"] = tok[f"qks_{t}_{c}"]

            @bld.step("tensor", after=(tok[f"elu2_g{gch}_0"], tok[f"elu2_g{gch}_1"]),
                      name=f"Smm_{t}_{c}")
            def _(e, b, _t=t, _c=c):
                last = None
                for p in range(DC):
                    first = (_t == 0 and _c == 0 and p % 4 == 0)
                    lastg = (_t == NTL - 1 and _c == CPT - 1 and p % 4 == 3)
                    last = e.matmul(ps_S[:, p, :],
                                    k2p[:, _c % 2, p * 128:(p + 1) * 128],
                                    q2p[:, _c % 2, p * 128:(p + 1) * 128],
                                    start=first, stop=lastg, skip_group_check=True)
                return last
            tok[f"Smm_{t}_{c}"] = bld.tk("tensor")
            tok[f"Smm_g{gch}"] = tok[f"Smm_{t}_{c}"]

        tok[f"xnst_{t}"] = bld.reserve("xst", DC)

        @bld.step("sync", after=(tok[f"tpeb_{t}_{CPT-1}"],), name=f"xnst_{t}")
        def _(e, b, _t=t):
            for fi in range(DC):
                b.dma(e, "xst", xn1_hbm[fi, :, _t * TPT:(_t + 1) * TPT],
                      xnfm[:, fi, :])
            return None

    tok["S_done"] = tok[f"Smm_{NTL-1}_{CPT-1}"]

    @bld.step("vector", after=(tok["S_done"],), name="Sev")
    def _(e, b):
        return e.tensor_copy(S_snd.rearrange("p (a c) -> p a c", c=128),
                             ps_S[:, :, :])
    tok["Sev"] = bld.tk("vector")

    rd1 = [None] * 8
    rd1[1] = (0, 1)
    rd2 = [None] * 8
    rd2[2] = (0, 2)

    @bld.step("gpsimd", after=(tok["Sev"], tok["setup_ld"]), name="xchS")
    def _(e, b):
        e.load_library(libcfg.remote_dma)
        e.remote_dma_broadcast(
            agS[:, 0, :], S_snd[:, :], rsem, lsem, rdests=rd1,
        ).then_inc(psem, 1)
        e.wait_ge(psem, 1)
        e.trigger_dma(count=1)
        e.wait_ge(lsem, 16)
        e.wait_ge(rsem, 2)
        return e.nop()
    tok["xchS"] = bld.tk("gpsimd")

    @bld.step("vector", after=(tok["xchS"],), name="Ssum1")
    def _(e, b):
        return e.tensor_tensor(agS[:, 1, :], S_snd[:, :], agS[:, 0, :],
                               ALU.add)
    tok["Ssum1"] = bld.tk("vector")

    @bld.step("gpsimd", after=(tok["Ssum1"],), name="xchS2")
    def _(e, b):
        e.remote_dma_broadcast(
            agS[:, 2, :], agS[:, 1, :], rsemS2, lsemS2, rdests=rd2,
        ).then_inc(psem, 1)
        e.wait_ge(psem, 2)
        e.trigger_dma(count=1)
        e.wait_ge(lsemS2, 16)
        e.wait_ge(rsemS2, 2)
        return e.nop()
    tok["xchS2"] = bld.tk("gpsimd")

    @bld.step("vector", after=(tok["xchS2"],), name="Ssum")
    def _(e, b):
        e.tensor_tensor(S_loc[:, :], agS[:, 1, :], agS[:, 2, :], ALU.add)
        e.drain()
        Sv = S_loc.rearrange("p (c f) -> p c f", f=128)
        for p in range(DC):
            e.memset(Sv[0:64, p, 64:128], 0.0)
            e.drain()
            e.memset(Sv[64:128, p, 0:64], 0.0)
            e.drain()
        return e.tensor_copy(S_bf[:, :, :], Sv)
    tok["Ssum"] = bld.tk("vector")

    p1_ps.close()
    p2_ps = ExitStack()
    ps_rb = p2_ps.enter_context(nc.psum_tensor("ps_rb", [128, 512], F32))
    ps_out = p2_ps.enter_context(nc.psum_tensor("ps_out", [128, 512], F32))
    ps_rtt = p2_ps.enter_context(nc.psum_tensor("ps_rtt", [128, 4, 128], F32))
    ps_lg = p2_ps.enter_context(nc.psum_tensor("ps_lg", [16, 128], F32))

    w1ctx.close()
    w2ctx = ExitStack()
    weffv = w2ctx.enter_context(nc.sbuf_tensor("weffv", [128, DC, D], BF16))
    wo_sb = w2ctx.enter_context(nc.sbuf_tensor("wo_sb", [128, DC, D], BF16))

    tok["p2_wload"] = bld.reserve("ld", DC)

    @bld.step("sync", after=(tok[f"fstd_2_{DC-1}"], tok[f"fstd_2_{DC-2}"],
                             tok[f"q2mm_{NTL-1}_{CPT-1}_1"]),
              name="p2_wload")
    def _(e, b):
        for fi in range(DC):
            b.dma(e, "ld", weffv[:, fi, :], weff_hbm[2, :, fi, :])
        return None

    tok["wo_ld"] = bld.reserve("wld", 1)

    @bld.step("gpsimd", after=(tok["xchS"], tok[f"q2mm_{NTL-1}_{CPT-1}_1"]),
              name="wo_ld")
    def _(e, b):
        b.dma(e, "wld", wo_sb[:, :, :], wo_d.rearrange("c p d -> p c d"))
        return None

    @bld.step("vector", after=(tok[f"qks_{NTL-1}_{CPT-1}"],), name="rprep")
    def _(e, b):
        e.tensor_scalar(r_tm[:, :, :], qks[:, :, :], EPSQ, None, ALU.add)
        e.drain()
        return e.reciprocal(r_tm[:, :, :], r_tm[:, :, :])
    tok["rprep"] = bld.tk("vector")

    for g in range(NCH // 4):
        @bld.step("tensor", after=(tok["rprep"], tok.get(f"rTe_{g-1}"),
                                   tok["Ssum"]), name=f"rT_{g}")
        def _(e, b, _g=g):
            last = None
            for k in range(4):
                ch = _g * 4 + k
                last = e.transpose(ps_rtt[0:16, k, :], r_tm[:, ch, :],
                                   ident_f[:, :])
            return last
        tok[f"rT_{g}"] = bld.tk("tensor")

        @bld.step("vector", after=(tok[f"rT_{g}"],), name=f"rTe_{g}")
        def _(e, b, _g=g):
            last = None
            for k in range(4):
                ch = _g * 4 + k
                last = e.tensor_copy(r_fm[:, ch * 128:(ch + 1) * 128],
                                     ps_rtt[0:16, k, :])
                e.drain()
            return last
        tok[f"rTe_{g}"] = bld.tk("vector")
    tok["r_fm"] = tok[f"rTe_{NCH//4-1}"]

    for t in range(NTL):
        tok[f"xl2_{t}"] = bld.reserve("ld", CPT + DC)

        @bld.step("sync", after=(tok.get(f"xm_{t-1}_{CPT-1}"),
                                 tok.get(f"v2mm_{t-1}_{DC-1}"),
                                 tok[f"xnst_{t}"], tok["Sev"],
                                 (tok[f"xnst_{NTL-1}"] if t == 0 else None),
                                 tok["p2_wload"]), name=f"xl2_{t}")
        def _(e, b, _t=t):
            for c in range(CPT):
                b.dma(e, "ld", x_t[:, c, :],
                      x_in[(_t * CPT + c) * 128:(_t * CPT + c + 1) * 128, :])
            for fi in range(DC):
                b.dma(e, "ld", xnfm[:, fi, :],
                      xn1_hbm[fi, :, _t * TPT:(_t + 1) * TPT])
            return None

        prev_v = tok.get(f"v2ev_{t-1}_{DC-1}")
        for fo in range(DC):
            @bld.step("tensor", after=(tok[f"xl2_{t}"], prev_v), name=f"v2mm_{t}_{fo}")
            def _(e, b, _t=t, _fo=fo):
                last = None
                for fi in range(DC):
                    last = e.matmul(ps_qv[:, :],
                                    weffv[:, fi, _fo * 128:(_fo + 1) * 128],
                                    xnfm[:, fi, :],
                                    start=(fi == 0), stop=(fi == DC - 1))
                return last
            tok[f"v2mm_{t}_{fo}"] = bld.tk("tensor")

            @bld.step("scalar", after=(tok[f"v2mm_{t}_{fo}"],
                                       tok.get(f"v2r_{t-1}_{fo}")),
                      name=f"v2ev_{t}_{fo}")
            def _(e, b, _fo=fo):
                return e.activation(v2fm[:, _fo, :], ps_qv[:, :], AF.Identity,
                                    bias=beffv_pp[:, _fo:_fo + 1], scale=1.0)
            tok[f"v2ev_{t}_{fo}"] = bld.tk("scalar")
            prev_v = tok[f"v2ev_{t}_{fo}"]

        for p in range(DC):
            @bld.step("tensor", after=(tok["r_fm"], tok[f"v2ev_{t}_{p}"],
                                       tok.get(f"v2r_{t}_{p-1}"),
                                       (tok.get(f"v2r_{t-1}_{DC-1}") if p == 0
                                        else None)),
                      name=f"rbmm_{t}_{p}")
            def _(e, b, _t=t, _p=p):
                return e.matmul(ps_rb[:, :], selp[:, _p, :],
                                r_fm[:, _t * TPT:(_t + 1) * TPT],
                                start=True, stop=True)
            tok[f"rbmm_{t}_{p}"] = bld.tk("tensor")

            if p >= 2:
                war_v2r = tok.get(f"outmm_{t}_{p-2}")
            else:
                war_v2r = tok.get(f"outmm_{t-1}_{DC-2+p}")

            @bld.step("vector", after=(tok[f"rbmm_{t}_{p}"], war_v2r),
                      name=f"v2r_{t}_{p}")
            def _(e, b, _p=p):
                return e.tensor_tensor(v2r[:, _p % 2, :], v2fm[:, _p, :],
                                       ps_rb[:, :], ALU.mult)
            tok[f"v2r_{t}_{p}"] = bld.tk("vector")

            @bld.step("tensor", after=(tok[f"v2r_{t}_{p}"], tok["Ssum"],
                                       tok.get(f"outev_{t}_{p-1}"),
                                       (tok.get(f"womm_{t-1}_{CPT-1}") if p == 0
                                        else None),
                                       (tok.get(f"outev_{t-1}_{DC-1}") if p == 0
                                        else None)),
                      name=f"outmm_{t}_{p}")
            def _(e, b, _p=p):
                return e.matmul(ps_out[:, :], S_bf[:, _p, :], v2r[:, _p % 2, :],
                                start=True, stop=True)
            tok[f"outmm_{t}_{p}"] = bld.tk("tensor")

            @bld.step("vector", after=(tok[f"outmm_{t}_{p}"],), name=f"outev_{t}_{p}")
            def _(e, b, _p=p):
                return e.tensor_copy(outsb[:, _p, :], ps_out[:, :])
            tok[f"outev_{t}_{p}"] = bld.tk("vector")
        tok[f"out_{t}"] = tok[f"outev_{t}_{DC-1}"]

        for c in range(CPT):
            @bld.step("tensor", after=(tok[f"out_{t}"], tok["bo_copy"],
                                       tok["wo_ld"],
                                       tok.get(f"y_{t}_{c-1}"),
                                       (tok.get(f"y_{t-1}_{CPT-1}") if c == 0
                                        else None)),
                      name=f"womm_{t}_{c}")
            def _(e, b, _c=c):
                last = None
                for half in range(2):
                    sl = slice(half * 512, (half + 1) * 512)
                    for ec in range(DC):
                        e.matmul(ps_q2[:, sl],
                                 outsb[:, ec, _c * 128:(_c + 1) * 128],
                                 wo_sb[:, ec, sl], start=(ec == 0), stop=False)
                    last = e.matmul(ps_q2[:, sl], onesb[0:1, :],
                                    beff_rows[0:1, 3, sl], start=False, stop=True)
                return last
            tok[f"womm_{t}_{c}"] = bld.tk("tensor")

            @bld.step("vector", after=(tok[f"womm_{t}_{c}"],
                                       tok.get(f"xn2_{t-1}_{c}"),
                                       tok.get(f"xmst_{t-1}_{c}")),
                      name=f"y_{t}_{c}")
            def _(e, b, _t=t, _c=c):
                ch = _t * CPT + _c
                e.tensor_scalar(vtmp[:, :], x_t[:, _c, :],
                                stats[:, 0, ch:ch + 1], stats[:, 1, ch:ch + 1],
                                ALU.subtract, ALU.mult)
                e.drain()
                return e.tensor_tensor(ybuf[:, _c, :], ps_q2[:, :], vtmp[:, :],
                                       ALU.add)
            tok[f"y_{t}_{c}"] = bld.tk("vector")

        @bld.step("scalar", after=(tok[f"y_{t}_{CPT-1}"],
                                   tok.get(f"var1_{NTL-1}") if t == 0
                                   else tok.get(f"var2_{t-1}")), name=f"sta_{t}")
        def _(e, b, _t=t):
            last = None
            for c in range(CPT):
                ch = _t * CPT + c
                e.activation(sjunk[:, :], ybuf[:, c, :], AF.Identity,
                             bias=0.0, scale=1.0, accum_out=stats[:, 6, ch:ch + 1])
                e.drain()
                last = e.activation(sjunk[:, :], ybuf[:, c, :], AF.Square,
                                    bias=0.0, scale=1.0,
                                    accum_out=stats[:, 7, ch:ch + 1])
                e.drain()
            return last
        tok[f"sta_{t}"] = bld.tk("scalar")

        @bld.step("vector", after=(tok[f"sta_{t}"],), name=f"vara_{t}")
        def _(e, b, _t=t):
            sl = slice(_t * CPT, (_t + 1) * CPT)
            e.tensor_scalar(stats[:, 2, sl], stats[:, 6, sl], 1.0 / D, None,
                            ALU.mult)
            e.drain()
            e.tensor_scalar(stt1[:, 0:CPT], stats[:, 7, sl], 1.0 / D, None, ALU.mult)
            e.drain()
            e.tensor_tensor(stt2[:, 0:CPT], stats[:, 2, sl], stats[:, 2, sl],
                            ALU.mult)
            e.drain()
            e.tensor_tensor(stt1[:, 0:CPT], stt1[:, 0:CPT], stt2[:, 0:CPT],
                            ALU.subtract)
            e.drain()
            return e.tensor_scalar(stt1[:, 0:CPT], stt1[:, 0:CPT], 1.0, EPSA,
                                   ALU.mult, ALU.add)
        tok[f"vara_{t}"] = bld.tk("vector")

        @bld.step("scalar", after=(tok[f"vara_{t}"],), name=f"sqa_{t}")
        def _(e, b):
            return e.activation(stt2[:, 0:CPT], stt1[:, 0:CPT], AF.Sqrt,
                                bias=0.0, scale=1.0)
        tok[f"sqa_{t}"] = bld.tk("scalar")

        @bld.step("vector", after=(tok[f"sqa_{t}"],), name=f"rsa_{t}")
        def _(e, b, _t=t):
            return e.reciprocal(stats[:, 3, _t * CPT:(_t + 1) * CPT], stt2[:, 0:CPT])
        tok[f"rsa_{t}"] = bld.tk("vector")

        for c in range(CPT):
            @bld.step("vector", after=(tok[f"rsa_{t}"], tok["gb_done"]),
                      name=f"xm_{t}_{c}")
            def _(e, b, _t=t, _c=c):
                ch = _t * CPT + _c
                e.tensor_scalar(zbuf[:, :], ybuf[:, _c, :],
                                stats[:, 2, ch:ch + 1], stats[:, 3, ch:ch + 1],
                                ALU.subtract, ALU.mult)
                e.drain()
                e.tensor_tensor(zbuf[:, :], zbuf[:, :], gam_b[:, :], ALU.mult)
                e.drain()
                e.tensor_tensor(zbuf[:, :], zbuf[:, :], bet_b[:, :], ALU.add)
                e.drain()
                return e.tensor_tensor(ybuf[:, _c, :], x_t[:, _c, :],
                                       zbuf[:, :], ALU.add)
            tok[f"xm_{t}_{c}"] = bld.tk("vector")

            tok[f"xmst_{t}_{c}"] = bld.reserve("xmst", 1)

            @bld.step("sync", after=(tok[f"xm_{t}_{c}"],), name=f"xmst_{t}_{c}")
            def _(e, b, _t=t, _c=c):
                ch = _t * CPT + _c
                b.dma(e, "xmst", xmid_hbm[ch * 128:(ch + 1) * 128, :],
                      ybuf[:, _c, :])
                return None

        @bld.step("scalar", after=(tok[f"xm_{t}_{CPT-1}"],), name=f"st2_{t}")
        def _(e, b, _t=t):
            last = None
            for c in range(CPT):
                ch = _t * CPT + c
                e.activation(sjunk[:, :], ybuf[:, c, :], AF.Identity,
                             bias=0.0, scale=1.0, accum_out=stats[:, 6, ch:ch + 1])
                e.drain()
                last = e.activation(sjunk[:, :], ybuf[:, c, :], AF.Square,
                                    bias=0.0, scale=1.0,
                                    accum_out=stats[:, 7, ch:ch + 1])
                e.drain()
            return last
        tok[f"st2_{t}"] = bld.tk("scalar")

        @bld.step("vector", after=(tok[f"st2_{t}"],), name=f"var2_{t}")
        def _(e, b, _t=t):
            sl = slice(_t * CPT, (_t + 1) * CPT)
            e.tensor_scalar(stats[:, 4, sl], stats[:, 6, sl], 1.0 / D, None,
                            ALU.mult)
            e.drain()
            e.tensor_scalar(stt1[:, 0:CPT], stats[:, 7, sl], 1.0 / D, None, ALU.mult)
            e.drain()
            e.tensor_tensor(stt2[:, 0:CPT], stats[:, 4, sl], stats[:, 4, sl],
                            ALU.mult)
            e.drain()
            e.tensor_tensor(stt1[:, 0:CPT], stt1[:, 0:CPT], stt2[:, 0:CPT],
                            ALU.subtract)
            e.drain()
            return e.tensor_scalar(stt1[:, 0:CPT], stt1[:, 0:CPT], 1.0, EPS1,
                                   ALU.mult, ALU.add)
        tok[f"var2_{t}"] = bld.tk("vector")

        @bld.step("scalar", after=(tok[f"var2_{t}"],), name=f"sq2_{t}")
        def _(e, b):
            return e.activation(stt2[:, 0:CPT], stt1[:, 0:CPT], AF.Sqrt,
                                bias=0.0, scale=1.0)
        tok[f"sq2_{t}"] = bld.tk("scalar")

        @bld.step("vector", after=(tok[f"sq2_{t}"],), name=f"rs2_{t}")
        def _(e, b, _t=t):
            return e.reciprocal(stats[:, 5, _t * CPT:(_t + 1) * CPT], stt2[:, 0:CPT])
        tok[f"rs2_{t}"] = bld.tk("vector")

        for c in range(CPT):
            gch = t * CPT + c

            @bld.step("vector", after=(tok[f"rs2_{t}"],
                                       tok.get(f"rtt2_g{gch-2}"),
                                       tok.get(f"x2st_g{gch-2}")),
                      name=f"xn2_{t}_{c}")
            def _(e, b, _t=t, _c=c):
                ch = _t * CPT + _c
                x2fc = tmin if _c % 2 == 0 else texp
                e.tensor_scalar(x2fc[:, :], ybuf[:, _c, :],
                                stats[:, 4, ch:ch + 1], stats[:, 5, ch:ch + 1],
                                ALU.subtract, ALU.mult)
                e.drain()
                return e.tensor_copy(xnb[:, _c % 2, :], x2fc[:, :])
            tok[f"xn2_{t}_{c}"] = bld.tk("vector")

            tok[f"x2st_{t}_{c}"] = bld.reserve("x2st", 2)
            tok[f"x2st_g{gch}"] = tok[f"x2st_{t}_{c}"]

            @bld.step("sync", after=(tok[f"xn2_{t}_{c}"],), name=f"x2st_{t}_{c}")
            def _(e, b, _t=t, _c=c):
                ch = _t * CPT + _c
                xv = xn2_hbm.rearrange("(t k) d -> t k d", k=2)
                b.dma(e, "x2st", xv[ch * 128:(ch + 1) * 128, 0, :],
                      xnb[:, _c % 2, :])
                b.dma(e, "x2st", xv[ch * 128:(ch + 1) * 128, 1, :],
                      xnb[:, _c % 2, :])
                return None

            @bld.step("tensor", after=(tok[f"xn2_{t}_{c}"],
                                       tok.get(f"lgtev_g{gch-1}"),
                                       tok.get(f"lgmm_g{gch-1}")),
                      name=f"rtt_{t}_{c}")
            def _(e, b, _c=c):
                x2fc = tmin if _c % 2 == 0 else texp
                last = None
                for k in range(4):
                    last = e.transpose(ps_rtt[:, k, :],
                                       x2fc[:, k * 128:(k + 1) * 128],
                                       ident_f[:, :])
                return last
            tok[f"rtt_{t}_{c}"] = bld.tk("tensor")

            @bld.step("vector", after=(tok[f"rtt_{t}_{c}"],), name=f"rtev0_{t}_{c}")
            def _(e, b):
                return e.tensor_copy(x2fm[:, 0:4, :], ps_rtt[:, :, :])
            tok[f"rtev0_{t}_{c}"] = bld.tk("vector")

            @bld.step("tensor", after=(tok[f"rtev0_{t}_{c}"],), name=f"rtt2_{t}_{c}")
            def _(e, b, _c=c):
                x2fc = tmin if _c % 2 == 0 else texp
                last = None
                for k in range(4):
                    last = e.transpose(ps_rtt[:, k, :],
                                       x2fc[:, (4 + k) * 128:(5 + k) * 128],
                                       ident_f[:, :])
                return last
            tok[f"rtt2_{t}_{c}"] = bld.tk("tensor")
            tok[f"rtt2_g{gch}"] = tok[f"rtt2_{t}_{c}"]

            @bld.step("vector", after=(tok[f"rtt2_{t}_{c}"],), name=f"rtev_{t}_{c}")
            def _(e, b):
                return e.tensor_copy(x2fm[:, 4:8, :], ps_rtt[:, :, :])
            tok[f"rtev_{t}_{c}"] = bld.tk("vector")

            @bld.step("tensor", after=(tok[f"rtev_{t}_{c}"], tok["setup_ld"],
                                       tok.get(f"lgev_g{gch-1}")),
                      name=f"lgmm_{t}_{c}")
            def _(e, b):
                last = None
                for fi in range(DC):
                    last = e.matmul(ps_lg[:, :], wrt_sb[:, fi, :], x2fm[:, fi, :],
                                    start=(fi == 0), stop=(fi == DC - 1))
                return last
            tok[f"lgmm_{t}_{c}"] = bld.tk("tensor")
            tok[f"lgmm_g{gch}"] = tok[f"lgmm_{t}_{c}"]

            @bld.step("scalar", after=(tok[f"lgmm_{t}_{c}"],
                                       tok.get(f"lgt_g{gch-1}")),
                      name=f"lgev_{t}_{c}")
            def _(e, b):
                return e.activation(lgfm[:, :], ps_lg[:, :], AF.Identity,
                                    bias=rb_sb[:, 0:1], scale=1.0)
            tok[f"lgev_{t}_{c}"] = bld.tk("scalar")
            tok[f"lgev_g{gch}"] = tok[f"lgev_{t}_{c}"]

            @bld.step("tensor", after=(tok[f"lgev_{t}_{c}"],), name=f"lgt_{t}_{c}")
            def _(e, b):
                return e.transpose(ps_rtt[:, 0, 0:16], lgfm[:, :], ident_f[0:16, 0:16])
            tok[f"lgt_{t}_{c}"] = bld.tk("tensor")
            tok[f"lgt_g{gch}"] = tok[f"lgt_{t}_{c}"]

            @bld.step("vector", after=(tok[f"lgt_{t}_{c}"],), name=f"lgtev_{t}_{c}")
            def _(e, b, _t=t, _c=c):
                ch = _t * CPT + _c
                return e.tensor_copy(logits_tm[:, ch, :], ps_rtt[:, 0, 0:16])
            tok[f"lgtev_{t}_{c}"] = bld.tk("vector")
            tok[f"lgtev_g{gch}"] = tok[f"lgtev_{t}_{c}"]

    tok["p2_done"] = tok[f"lgtev_{NTL-1}_{CPT-1}"]

    w2ctx.close()
    actx.close()
    pctx.close()
    p2_ps.close()
    a_ps.close()

    rctx = ExitStack()
    oh_all = rctx.enter_context(nc.sbuf_tensor("oh_all", [32, E, 256], F32))
    scan_all = rctx.enter_context(nc.sbuf_tensor("scan_all", [32, E, 256], F32))
    kcmp = rctx.enter_context(nc.sbuf_tensor("kcmp", [32, 256], F32))
    kept1 = rctx.enter_context(nc.sbuf_tensor("kept1", [32, 256], F32))
    arr32 = rctx.enter_context(nc.sbuf_tensor("arr32", [32, 256], F32))
    goffp = rctx.enter_context(nc.sbuf_tensor("goffp", [32, E], F32))
    arr16 = rctx.enter_context(nc.sbuf_tensor("arr16", [16, 512], F32))
    den32 = rctx.enter_context(nc.sbuf_tensor("den32", [32, 128], F32))
    scr32 = rctx.enter_context(nc.sbuf_tensor("scr32", [32, 128], F32))
    r_ps = ExitStack()
    ps_r1 = r_ps.enter_context(nc.psum_tensor("ps_r1", [128, 128], F32))

    guard = (tok[f"x2st_{NTL-1}_{CPT-1}"], tok[f"x2st_{NTL-1}_{CPT-2}"],
             tok[f"xmst_{NTL-1}_{CPT-1}"], tok[f"xmst_{NTL-1}_{CPT-2}"],
             tok[f"xnst_{NTL-1}"], tok["p2_done"])

    for ch in range(NCH):
        tl, cc = ch // CPT, ch % CPT

        @bld.step("vector",
                  after=(tok[f"lgtev_{tl}_{cc}"], tok["setup_gp"]) +
                        (guard if ch == 0 else ()),
                  name=f"top2_{ch}")
        def _(e, b, _ch=ch):
            lg = logits_tm[:, _ch, :]
            e.tensor_reduce(v1c[:, _ch:_ch + 1], lg, AX.X, ALU.max)
            e.drain()
            e.tensor_scalar(sc16[:, :], lg, v1c[:, _ch:_ch + 1], None, ALU.is_ge)
            e.drain()
            e.scalar_tensor_tensor(sc16b[:, :], sc16[:, :], -float(2 ** 20),
                                   iotB[:, :], ALU.mult, ALU.add)
            e.drain()
            e.tensor_reduce(i1c[:, _ch:_ch + 1], sc16b[:, :], AX.X, ALU.min)
            e.drain()
            e.tensor_scalar(sc16[:, :], iot16[:, :], i1c[:, _ch:_ch + 1], None,
                            ALU.is_equal)
            e.drain()
            e.scalar_tensor_tensor(sc16b[:, :], sc16[:, :], -1e30, lg,
                                   ALU.mult, ALU.add)
            e.drain()
            e.tensor_reduce(v2c[:, _ch:_ch + 1], sc16b[:, :], AX.X, ALU.max)
            e.drain()
            e.tensor_scalar(sc16[:, :], sc16b[:, :], v2c[:, _ch:_ch + 1], None,
                            ALU.is_ge)
            e.drain()
            e.scalar_tensor_tensor(sc16b[:, :], sc16[:, :], -float(2 ** 20),
                                   iotB[:, :], ALU.mult, ALU.add)
            e.drain()
            e.tensor_reduce(i2c[:, _ch:_ch + 1], sc16b[:, :], AX.X, ALU.min)
            e.drain()
            return None
        tok[f"top2_{ch}"] = bld.tk("vector")

    @bld.step("vector", after=(tok[f"top2_{NCH-1}"],), name="dv")
    def _(e, b):
        return e.tensor_tensor(stt1[:, :], v2c[:, :], v1c[:, :], ALU.subtract)
    tok["dv"] = bld.tk("vector")

    @bld.step("scalar", after=(tok["dv"],), name="gexp")
    def _(e, b):
        return e.activation(stt2[:, :], stt1[:, :], AF.Exp, bias=0.0, scale=1.0)
    tok["gexp"] = bld.tk("scalar")

    @bld.step("vector", after=(tok["gexp"],), name="gates")
    def _(e, b):
        e.tensor_scalar(stt1[:, :], stt2[:, :], 1.0, None, ALU.add)
        e.drain()
        e.reciprocal(g1c[:, :], stt1[:, :])
        e.drain()
        return e.tensor_tensor(g2c[:, :], stt2[:, :], g1c[:, :], ALU.mult)
    tok["gates"] = bld.tk("vector")

    for k, (src, dst) in enumerate([(i1c, i1T), (i2c, i2T), (g1c, g1T),
                                    (g2c, g2T)]):
        @bld.step("tensor", after=(tok["gates"], tok.get(f"trTe_{k-1}")),
                  name=f"trT_{k}")
        def _(e, b, _s=src):
            return e.transpose(ps_r1[0:32, :], _s[:, :], ident_f[:, :])
        tok[f"trT_{k}"] = bld.tk("tensor")

        @bld.step("vector", after=(tok[f"trT_{k}"],), name=f"trTe_{k}")
        def _(e, b, _d=dst):
            return e.tensor_copy(_d[:, :], ps_r1[0:32, :])
        tok[f"trTe_{k}"] = bld.tk("vector")
    tok["trT"] = tok["trTe_3"]

    for ee in range(E):
        @bld.step("vector", after=(tok["trT"], tok.get(f"arrD_{ee-1}")),
                  name=f"ohE_{ee}")
        def _(e, b, _e=ee):
            e.tensor_scalar(oh_all[:, _e, 0:256:2], i1T[:, :], float(_e), None,
                            ALU.is_equal)
            e.drain()
            e.tensor_scalar(oh_all[:, _e, 1:256:2], i2T[:, :], float(_e), None,
                            ALU.is_equal)
            e.drain()
            e.scalar_tensor_tensor(arr32[:, :], siota[:, :], 1.0,
                                   oh_all[:, _e, :], ALU.add, ALU.mult)
            e.drain()
            e.tensor_scalar(arr32[:, :], arr32[:, :], -1.0, None, ALU.add)
            e.drain()
            return e.tensor_tensor_scan(scan_all[:, _e, :], oh_all[:, _e, :],
                                        zer32[:, :], 0.0, ALU.add, ALU.add)
        tok[f"ohE_{ee}"] = bld.tk("vector")

        tok[f"arrD_{ee}"] = bld.reserve("gld", 2)

        @bld.step("gpsimd", after=(tok[f"ohE_{ee}"], tok["xchS"]),
                  name=f"arrD_{ee}")
        def _(e, b):
            b.dma(e, "gld", arr16[:, 0:256], arr32[0:16, :])
            b.dma(e, "gld", arr16[:, 256:512], arr32[16:32, :])
            return None

        @bld.step("gpsimd", after=(tok[f"arrD_{ee}"],), name=f"spg_{ee}")
        def _(e, b, _e=ee):
            if _e == 0:
                e.load_library(libcfg.sparse_gather)
            return e.sparse_gather(comp_all[:, _e, :], arr16[:, :],
                                   num_found=counts_u[0:1, _e:_e + 1])
        tok[f"spg_{ee}"] = bld.tk("gpsimd")

    @bld.step("vector", after=(tok[f"spg_{E-1}"],), name="cvt")
    def _(e, b):
        e.memset(counts_fx[:, :], 0.0)
        e.drain()
        return e.tensor_copy(counts_fx[0:1, 0:E], counts_u[:, :])
    tok["cvt"] = bld.tk("vector")

    @bld.step("gpsimd", after=(tok["cvt"],), name="xchC")
    def _(e, b):
        e.load_library(libcfg.remote_dma)
        rank = nc.values_load(rk_sb[0:1, 0:1], engines=[ENG.Pool],
                              min_val=0, max_val=7)
        e.remote_dma_broadcast(
            counts_ag[:, ds(rank, 1), :], counts_fx[:, :], rsem2, lsem2,
            rdests=[(0, k) for k in range(NC_)],
        ).then_inc(psem, 1)
        e.wait_ge(psem, 3)
        e.trigger_dma(count=1)
        e.wait_ge(lsem2, 16)
        e.wait_ge(rsem2, 16)
        return e.nop()
    tok["xchC"] = bld.tk("gpsimd")

    @bld.step("vector", after=(tok["xchC"],), name="goffev")
    def _(e, b):
        e.tensor_scalar(goff_row[:, :], counts_ag[0:1, 0, 0:E],
                        maskLT_sb[0:1, 0:1], None, ALU.mult)
        for r in range(1, NC_):
            e.drain()
            e.scalar_tensor_tensor(goff_row[:, :], counts_ag[0:1, r, 0:E],
                                   maskLT_sb[0:1, r:r + 1], goff_row[:, :],
                                   ALU.mult, ALU.add)
        return None
    tok["goffev"] = bld.tk("vector")

    @bld.step("gpsimd", after=(tok["goffev"], tok["xchC"]), name="goffbc")
    def _(e, b):
        e.load_library(libcfg.mlp)
        return e.partition_broadcast(goffp[:, :], goff_row[0:1, 0:E],
                                     channels=32)
    tok["goffbc"] = bld.tk("gpsimd")

    for ee in range(E):
        @bld.step("tensor", after=(tok["goffev"], tok[f"ohE_{ee}"],
                                   tok.get(f"kept_{ee-1}")), name=f"offmm_{ee}")
        def _(e, b, _e=ee):
            return e.matmul(ps_r1[0:32, 0:1], tri32[:, :],
                            scan_all[:, _e, 255:256], start=True, stop=True)
        tok[f"offmm_{ee}"] = bld.tk("tensor")

        @bld.step("vector", after=(tok[f"offmm_{ee}"], tok["goffbc"]),
                  name=f"kept_{ee}")
        def _(e, b, _e=ee):
            e.tensor_scalar(kcmp[:, :], scan_all[:, _e, :], ps_r1[0:32, 0:1],
                            None, ALU.add)
            e.drain()
            e.tensor_scalar(kcmp[:, :], kcmp[:, :], goffp[0:32, _e:_e + 1],
                            float(CAPG), ALU.add, ALU.is_le)
            e.drain()
            e.tensor_tensor(kept1[:, :], kcmp[:, :], oh_all[:, _e, :], ALU.mult)
            e.drain()
            if _e == 0:
                return e.tensor_copy(kacc[:, :], kept1[:, :])
            return e.tensor_tensor(kacc[:, :], kacc[:, :], kept1[:, :], ALU.add)
        tok[f"kept_{ee}"] = bld.tk("vector")

    @bld.step("vector", after=(tok[f"kept_{E-1}"],), name="gfin")
    def _(e, b):
        e.tensor_tensor(den32[:, :], g1T[:, :], kacc[:, 0:256:2], ALU.mult)
        e.drain()
        e.tensor_tensor(scr32[:, :], g2T[:, :], kacc[:, 1:256:2], ALU.mult)
        e.drain()
        e.tensor_tensor(den32[:, :], den32[:, :], scr32[:, :], ALU.add)
        e.drain()
        e.tensor_scalar(den32[:, :], den32[:, :], EPSG, None, ALU.add)
        e.drain()
        e.reciprocal(den32[:, :], den32[:, :])
        e.drain()
        e.tensor_tensor(gslot[:, 0:256:2], g1T[:, :], den32[:, :], ALU.mult)
        e.drain()
        e.tensor_tensor(gslot[:, 1:256:2], g2T[:, :], den32[:, :], ALU.mult)
        e.drain()
        return e.tensor_tensor(gslot[:, :], gslot[:, :], kacc[:, :], ALU.mult)
    tok["gfin"] = bld.tk("vector")

    for gk, nmk in [(0, "g0k"), (1, "g1k")]:
        @bld.step("tensor", after=(tok["gfin"], tok.get("g0kE") if gk == 1
                                   else tok["trT"]), name=f"{nmk}T")
        def _(e, b, _gk=gk):
            return e.transpose(ps_r1[:, 0:32], gslot[:, _gk:256:2],
                               ident_f[0:32, 0:32])
        tok[f"{nmk}T"] = bld.tk("tensor")

        @bld.step("vector", after=(tok[f"{nmk}T"],), name=f"{nmk}E")
        def _(e, b, _gk=gk):
            dst = g0k if _gk == 0 else g1k
            return e.tensor_copy(dst[:, :], ps_r1[:, 0:32])
        tok[f"{nmk}E"] = bld.tk("vector")
    tok["gstore"] = tok["g1kE"]

    @bld.step("gpsimd", after=(tok["cvt"], tok["xchC"]), name="cbc")
    def _(e, b):
        e.load_library(libcfg.mlp)
        return e.partition_broadcast(counts_bc[:, :], counts_fx[0:1, 0:E],
                                     channels=16)
    tok["cbc"] = bld.tk("gpsimd")

    for ee in range(E):
        @bld.step("vector", after=(tok["cbc"], tok[f"spg_{ee}"],
                                   tok.get(f"sanI_{ee-1}")), name=f"sanV_{ee}")
        def _(e, b, _e=ee):
            e.tensor_scalar(mskf[:, :], wiota[:, :], counts_bc[:, _e:_e + 1], None,
                            ALU.is_lt)
            e.drain()
            return e.memset(csafe[:, :], DUMMY)
        tok[f"sanV_{ee}"] = bld.tk("vector")

        @bld.step("scalar", after=(tok[f"sanV_{ee}"],), name=f"sanM_{ee}")
        def _(e, b):
            return e.activation(mski[:, :], mskf[:, :], AF.Identity,
                                bias=0.0, scale=1.0)
        tok[f"sanM_{ee}"] = bld.tk("scalar")

        @bld.step("vector", after=(tok[f"sanM_{ee}"],), name=f"sanP_{ee}")
        def _(e, b, _e=ee):
            return e.copy_predicated(csafe[:, :], mski[:, :], comp_all[:, _e, :])
        tok[f"sanP_{ee}"] = bld.tk("vector")

        @bld.step("scalar", after=(tok[f"sanP_{ee}"],), name=f"sanI_{ee}")
        def _(e, b, _e=ee):
            return e.activation(slot_idx[0:16, _e, :], csafe[:, :], AF.Identity,
                                bias=0.0, scale=1.0)
        tok[f"sanI_{ee}"] = bld.tk("scalar")

    tok["idxrep"] = bld.reserve("gld", 1)

    @bld.step("gpsimd", after=(tok[f"sanI_{E-1}"], tok["cbc"]), name="idxrep")
    def _(e, b):
        b.dma(e, "gld", slot_idx[16:32, :, :], slot_idx[0:16, :, :])
        return None

    if DBG:
        tok["dbgr"] = bld.reserve("dbg", 13)

        @bld.step("scalar", after=(tok["gfin"], tok["goffev"], tok["idxrep"],
                                   tok["gates"], tok["Ssum"], tok["rprep"]),
                  name="dbg_route")
        def _(e, b):
            b.dma(e, "dbg", dbg["logits"], logits_tm[:, :, :])
            b.dma(e, "dbg", dbg["route"][:, 0, :], v1c[:, :])
            b.dma(e, "dbg", dbg["route"][:, 1, :], v2c[:, :])
            b.dma(e, "dbg", dbg["route"][:, 2, :], i1c[:, :])
            b.dma(e, "dbg", dbg["route"][:, 3, :], i2c[:, :])
            b.dma(e, "dbg", dbg["route"][:, 4, :], g1c[:, :])
            b.dma(e, "dbg", dbg["route"][:, 5, :], g2c[:, :])
            b.dma(e, "dbg", dbg["counts"], counts_ag[0:1, :, 0:E])
            b.dma(e, "dbg", dbg["goff"], goff_row[:, :])
            b.dma(e, "dbg", dbg["S"], S_bf[:, :, :].rearrange("p a b -> p (a b)"))
            b.dma(e, "dbg", dbg["qks"], qks[:, :, :])
            b.dma(e, "dbg", dbg["gslot"], gslot[:, :])
            b.dma(e, "dbg", dbg["comp"], comp_all[:, :, :])
            return None
        tok["dbg_route"] = bld.tk("scalar")

    rctx.close()
    r_ps.close()
    mctx = ExitStack()

    def msb(name, shape, dtype):
        return mctx.enter_context(nc.sbuf_tensor(name, shape, dtype))

    P_rows = msb("P_rows", [128, DC, D], BF16)
    w1s = msb("w1s", [128, 2, 2, DC, 128], BF16)
    w2s = msb("w2s", [128, 2, NF + 1, 512], BF16)
    H_sb = msb("H_sb", [128, NF, CAP], BF16)
    O_sb = msb("O_sb", [128, NJ, D], BF16)
    m_ps = ExitStack()
    ps_h = [m_ps.enter_context(nc.psum_tensor(f"ps_h{i}", [128, CAP], F32))
            for i in range(2)]
    ps_o = [m_ps.enter_context(nc.psum_tensor(f"ps_o{i}", [128, 512], F32))
            for i in range(2)]
    ps_tp2 = m_ps.enter_context(nc.psum_tensor("ps_tp2", [128, 2, DC, 128],
                                               BF16))

    @bld.step("vector", after=(tok["gstore"], tok[f"kept_{E-1}"]),
              name="w2aug_zero")
    def _(e, b):
        return e.memset(w2s[:, :, NF, :], 0.0)
    tok["w2aug_zero"] = bld.tk("vector")

    last_x2st = tok[f"x2st_{NTL-1}_{CPT-1}"]
    last_x2st2 = tok[f"x2st_{NTL-1}_{CPT-2}"]

    for ee in range(E):
        tok[f"pg_{ee}"] = bld.reserve("moe", 1)

        @bld.step("gpsimd",
                  after=(tok["idxrep"], last_x2st, last_x2st2,
                         tok["zero_osl"], tok.get(f"Amm_{ee-1}_{NF-1}"),
                         (tok["gstore"] if ee == 0 else None),
                         (tok[f"kept_{E-1}"] if ee == 0 else None)),
                  name=f"pg_{ee}")
        def _(e, b, _e=ee):
            e.dma_gather(P_rows[:, :, :], xn2_hbm[0:2 * T + 128, :],
                         slot_idx[:, _e, :], CAP, CAP, D,
                         transpose=False).then_inc(b.seg_sem("moe"), 16)
            return None

        for c in range(DC):
            @bld.step("tensor",
                      after=(tok[f"pg_{ee}"], tok.get(f"tpe_{ee}_{c-2}"),
                             (tok.get(f"Amm_{ee-1}_{NF-1}") if c == 0 else None),
                             (tok["gstore"] if (ee == 0 and c == 0) else None),
                             (tok[f"kept_{E-1}"] if (ee == 0 and c == 0)
                              else None)),
                      name=f"tp_{ee}_{c}")
            def _(e, b, _c=c):
                last = None
                for dc in range(DC):
                    last = e.transpose(ps_tp2[:, _c % 2, dc, :],
                                       P_rows[:, _c, dc * 128:(dc + 1) * 128],
                                       ident_b[:, :])
                return last
            tok[f"tp_{ee}_{c}"] = bld.tk("tensor")

            @bld.step("vector", after=(tok[f"tp_{ee}_{c}"],
                                       (tok.get(f"Amm_{ee-1}_{NF-1}") if c == 0
                                        else None)),
                      name=f"tpe_{ee}_{c}")
            def _(e, b, _c=c):
                last = None
                for dc in range(DC):
                    last = e.tensor_copy(
                        P_rows[:, _c, dc * 128:(dc + 1) * 128],
                        ps_tp2[:, _c % 2, dc, :])
                return last
            tok[f"tpe_{ee}_{c}"] = bld.tk("vector")

        def decl_w2l(_ee, _h, war):
            tok[f"w2l_{_ee}_{_h}"] = bld.reserve("wld", 2)

            @bld.step("gpsimd", after=(tok["w2aug_zero"], war),
                      name=f"w2l_{_ee}_{_h}")
            def _(e, b, _e=_ee, _hh=_h):
                hb = (2 * _e + _hh) % 2
                b.dma(e, "wld", w2s[:, hb, 0:NF, :],
                      w2_d[_e, :, :, _hh * 512:(_hh + 1) * 512].rearrange(
                          "f p d -> p f d"))
                b.dma(e, "wld", w2s[0:1, hb, NF, :],
                      b2_d[_e:_e + 1, _hh * 512:(_hh + 1) * 512])
                return None

        decl_w2l(ee, 0, tok.get(f"Bmm_{ee-1}_0_{NJ-1}"))
        decl_w2l(ee, 1, tok.get(f"Bmm_{ee-1}_1_{NJ-1}"))

        def decl_w1l(_ee, _grp, w1war):
            tok[f"w1l_{_ee}_{_grp}"] = bld.reserve("wld", 1)

            @bld.step("gpsimd", after=(w1war,
                                       (tok["w2aug_zero"] if _ee == 0 and
                                        _grp < 2 else None)),
                      name=f"w1l_{_ee}_{_grp}")
            def _(e, b, _e=_ee, _g=_grp):
                b.dma(e, "wld", w1s[:, _g % 2, :, :, :],
                      w1_d[_e, 2 * _g:2 * _g + 2].rearrange(
                          "f p c m -> p f c m"))
                return None

        for grp in range(2):
            decl_w1l(ee, grp, tok.get(f"Amm_{ee-1}_{NF-4+2*grp+1}"))

        for f in range(NF):
            grp = f // 2
            if f >= 2:
                psh_war = tok.get(f"Aev_{ee}_{f-2}")
            else:
                psh_war = tok.get(f"Aev_{ee-1}_{NF-2+f}")

            @bld.step("tensor", after=(tok[f"w1l_{ee}_{grp}"],
                                       tok[f"tpe_{ee}_{DC-1}"],
                                       psh_war,
                                       (tok.get(f"Bmm_{ee-1}_1_{NJ-1}") if f == 0
                                        else None),
                                       (tok[f"kept_{E-1}"] if (ee == 0 and f == 0)
                                        else None),
                                       (tok["gstore"] if (ee == 0 and f == 0)
                                        else None)),
                      name=f"Amm_{ee}_{f}")
            def _(e, b, _e=ee, _f=f):
                gp = (_f // 2) % 2
                for dc in range(DC):
                    e.matmul(ps_h[_f % 2][:, 0:512], w1s[:, gp, _f % 2, dc, :],
                             P_rows[:, 0:4, dc * 128:(dc + 1) * 128],
                             start=(dc == 0), stop=False,
                             skip_group_check=True)
                    e.matmul(ps_h[_f % 2][:, 512:1024], w1s[:, gp, _f % 2, dc, :],
                             P_rows[:, 4:8, dc * 128:(dc + 1) * 128],
                             start=(dc == 0), stop=(dc == DC - 1),
                             skip_group_check=True)
                return None
            tok[f"Amm_{ee}_{f}"] = bld.tk("tensor")

            @bld.step("scalar", after=(tok[f"Amm_{ee}_{f}"],), name=f"Aev_{ee}_{f}")
            def _(e, b, _e=ee, _f=f):
                return e.activation(H_sb[:, _f, :], ps_h[_f % 2][:, :], AF.Gelu,
                                    bias=b1pp[:, _e, _f:_f + 1], scale=1.0)
            tok[f"Aev_{ee}_{f}"] = bld.tk("scalar")

            if f % 2 == 1 and f // 2 + 2 < NF // 2:
                decl_w1l(ee, f // 2 + 2, tok[f"Amm_{ee}_{f}"])

        for hh in range(2):
            for j in range(NJ):
                q = hh * NJ + j
                if q >= 2:
                    po_war = tok.get(f"Bev_{ee}_{(q-2)//NJ}_{(q-2)%NJ}")
                else:
                    po_war = tok.get(f"Bev_{ee-1}_{(2*NJ-2+q)//NJ}_"
                                     f"{(2*NJ-2+q)%NJ}")

                @bld.step("tensor", after=(tok[f"Aev_{ee}_{NF-1}"],
                                           tok[f"w2l_{ee}_{hh}"], po_war),
                          name=f"Bmm_{ee}_{hh}_{j}")
                def _(e, b, _e=ee, _h=hh, _j=j, _q=q):
                    jp = _q % 2
                    hb = _h % 2
                    for f in range(NF):
                        e.matmul(ps_o[jp][:, :],
                                 H_sb[:, f, _j * 128:(_j + 1) * 128],
                                 w2s[:, hb, f, :], start=(f == 0), stop=False,
                                 skip_group_check=True)
                    return e.matmul(ps_o[jp][:, :], onescol0[:, :],
                                    w2s[:, hb, NF, :], start=False, stop=True,
                                    skip_group_check=True)
                tok[f"Bmm_{ee}_{hh}_{j}"] = bld.tk("tensor")

                @bld.step("vector", after=(tok[f"Bmm_{ee}_{hh}_{j}"],
                                           tok.get(f"scat_{ee-1}")),
                          name=f"Bev_{ee}_{hh}_{j}")
                def _(e, b, _e=ee, _h=hh, _j=j, _q=q):
                    return e.tensor_copy(
                        O_sb[:, _j, _h * 512:(_h + 1) * 512], ps_o[_q % 2][:, :])
                tok[f"Bev_{ee}_{hh}_{j}"] = bld.tk("vector")

        tok[f"scat_{ee}"] = bld.reserve("moe", 1)

        @bld.step("gpsimd", after=(tok[f"Bev_{ee}_1_{NJ-1}"], tok["zero_osl"]),
                  name=f"scat_{ee}")
        def _(e, b, _e=ee):
            e.dma_scatter_add(osl_hbm[0:SLOTS + 128, :], O_sb[:, :, :],
                              slot_idx[:, _e, :], CAP, CAP, D).then_inc(
                                  b.seg_sem("moe"), 16)
            return None

    mctx.close()
    m_ps.close()
    cctx = ExitStack()
    G_sb = cctx.enter_context(nc.sbuf_tensor("G_sb", [128, 2, 2, D], BF16))
    xmc = cctx.enter_context(nc.sbuf_tensor("xmc", [128, 2, D], F32))
    outc = cctx.enter_context(nc.sbuf_tensor("outc", [128, 2, D], F32))

    for ch in range(NCH):
        par = ch % 2
        tok[f"gl_{ch}"] = bld.reserve("ld", 2)

        @bld.step("sync", after=(tok[f"scat_{E-1}"],
                                 (tok[f"scat_{E-2}"] if ch == 0 else None),
                                 (tok[f"xmst_{NTL-1}_{CPT-1}"] if ch == 0 else None),
                                 (tok[f"xmst_{NTL-1}_{CPT-2}"] if ch == 0 else None),
                                 tok.get(f"comb_{ch-2}")),
                  name=f"gl_{ch}")
        def _(e, b, _ch=ch, _p=par):
            b.dma(e, "ld", G_sb[:, _p, :, :],
                  osl_hbm[_ch * 256:(_ch + 1) * 256, :].rearrange(
                      "(p k) d -> p k d", k=2))
            b.dma(e, "ld", xmc[:, _p, :], xmid_hbm[_ch * 128:(_ch + 1) * 128, :])
            return None

        @bld.step("vector", after=(tok[f"gl_{ch}"], tok.get(f"ost_{ch-2}")),
                  name=f"comb_{ch}")
        def _(e, b, _ch=ch, _p=par):
            e.tensor_scalar(outc[:, _p, :], G_sb[:, _p, 0, :],
                            g0k[:, _ch:_ch + 1], None, ALU.mult)
            e.drain()
            e.scalar_tensor_tensor(outc[:, _p, :], G_sb[:, _p, 1, :],
                                   g1k[:, _ch:_ch + 1], outc[:, _p, :],
                                   ALU.mult, ALU.add)
            e.drain()
            return e.tensor_tensor(outc[:, _p, :], outc[:, _p, :], xmc[:, _p, :],
                                   ALU.add)
        tok[f"comb_{ch}"] = bld.tk("vector")

        tok[f"ost_{ch}"] = bld.reserve("ost", 1)

        @bld.step("scalar", after=(tok[f"comb_{ch}"],), name=f"ost_{ch}")
        def _(e, b, _ch=ch, _p=par):
            b.dma(e, "ost", y_out[_ch * 128:(_ch + 1) * 128, :], outc[:, _p, :])
            return None
        tok[f"ost_s{ch}"] = bld.tk("scalar")

    if DBG:
        tok["dbg2"] = bld.reserve("dbg", 3)

        @bld.step("scalar", after=(tok[f"ost_s{NCH-1}"],), name="dbg_tail")
        def _(e, b):
            b.dma(e, "dbg", dbg["xmid"], xmid_hbm[0:T, :])
            b.dma(e, "dbg", dbg["xn2"], xn2_hbm.rearrange(
                "(t k) d -> t k d", k=2)[0:T, 0, :])
            b.dma(e, "dbg", dbg["weff"], weff_hbm[:, :, :, :])
            return None
        tok["dbg_tail"] = bld.tk("scalar")

    @bld.step("gpsimd", after=(tok[f"ost_s{NCH-1}"],) +
              ((tok["dbg_tail"],) if DBG else ()), name="fin")
    def _(e, b):
        b.wait_all(e, "ost")
        if DBG:
            b.wait_all(e, "dbg")
        return e.nop()

    build_nc.marks = {}
    for eng in ENGS:
        for (fn, waits, name, gq) in bld.steps[eng]:
            build_nc.marks[name] = gq
    bld.emit()
    cctx.close()
    gctx.close()
    return nc


def _prep(inputs):
    f32 = np.float32
    ins = {k: np.asarray(v) for k, v in inputs.items()}
    x = np.ascontiguousarray(ins["L0_attn_x"].astype(f32).reshape(2 * 16384, D))

    def tile_lhsT(w):
        return np.ascontiguousarray(
            np.asarray(w).astype(f32).reshape(DC, 128, DC, 128)
            .transpose(0, 2, 1, 3))

    def rows8(w):
        return np.ascontiguousarray(np.asarray(w).astype(f32).reshape(DC, 128, D))

    def pp(bvec):
        return np.asarray(bvec).astype(f32).reshape(DC, 128).T

    common = {
        "wqT": tile_lhsT(np.asarray(ins["L0_attn_Wq"]).astype(f32).T),
        "wkT": tile_lhsT(np.asarray(ins["L0_attn_Wk"]).astype(f32).T),
        "wvT": tile_lhsT(np.asarray(ins["L0_attn_Wv"]).astype(f32).T),
        "wq2": rows8(ins["L0_attn_Wq2"]),
        "wk2": rows8(ins["L0_attn_Wk2"]),
        "wv2": rows8(ins["L0_attn_Wv2"]),
        "wo": rows8(ins["L0_attn_Wo"]),
        "bqkv": np.ascontiguousarray(np.concatenate(
            [pp(ins["L0_attn_bq"]), pp(ins["L0_attn_bk"]),
             pp(ins["L0_attn_bv"]), np.zeros((128, DC), f32)], axis=1)),
        "rows4": np.ascontiguousarray(np.stack(
            [np.asarray(ins["L0_attn_bq2"]).astype(f32),
             np.asarray(ins["L0_attn_bk2"]).astype(f32),
             np.asarray(ins["L0_attn_bv2"]).astype(f32),
             np.asarray(ins["L0_attn_bo"]).astype(f32)])),
        "gamma_row": np.ascontiguousarray(
            np.asarray(ins["L0_attn_gamma"]).astype(f32)[None, :]),
        "beta_row": np.ascontiguousarray(
            np.asarray(ins["L0_attn_beta"]).astype(f32)[None, :]),
        "wrt": np.ascontiguousarray(
            np.asarray(ins["L0_ffn_router_w"]).astype(f32).T
            .reshape(DC, 128, E).transpose(1, 0, 2)),
        "rb_col": np.ascontiguousarray(
            np.asarray(ins["L0_ffn_router_b"]).astype(f32)[:, None]),
        "w1t": np.ascontiguousarray(
            np.asarray(ins["L0_ffn_w1"]).astype(f32)
            .reshape(E, DC, 128, NF, 128).transpose(0, 3, 2, 1, 4)),
        "w2t": np.ascontiguousarray(
            np.asarray(ins["L0_ffn_w2"]).astype(f32).reshape(E, NF, 128, D)),
        "b1_pp": np.ascontiguousarray(
            np.asarray(ins["L0_ffn_b1"]).astype(f32)
            .reshape(E, NF, 128).transpose(2, 0, 1)),
        "b2_rows": np.ascontiguousarray(np.asarray(ins["L0_ffn_b2"]).astype(f32)),
    }
    in_maps = []
    for c in range(NC_):
        m = dict(common)
        m["x"] = np.ascontiguousarray(x[c * T:(c + 1) * T])
        ml = np.zeros((1, NC_), f32)
        ml[0, :c] = 1.0
        m["maskLT"] = ml
        m["rk4"] = np.array([[c % 4]], np.uint32)
        in_maps.append(m)
    return in_maps


_nc_cache = {}


def kernel(**inputs):
    in_maps = _prep(inputs)
    key = ("dbg" if DBG else "std", TRUNC)
    if key not in _nc_cache:
        nc = build_nc()
        lower_extended_insts(nc)
        _nc_cache[key] = nc
    nc = _nc_cache[key]
    res = run_bass_kernel_spmd(nc, in_maps, core_ids=list(range(NC_)))
    y = np.concatenate([res.results[c]["y"] for c in range(NC_)], axis=0)
    out = np.ascontiguousarray(y.reshape(2, 16384, D).astype(np.float32))
    if DBG:
        kernel._dbg = res.results
    return out
